# version 11
# baseline (speedup 1.0000x reference)
import math
from contextlib import ExitStack

import numpy as np
import concourse.bass as bass
import concourse.mybir as mybir
import concourse.bass_utils as _bass_utils
from concourse.bass_utils import run_bass_kernel_spmd

if not getattr(_bass_utils, "_ldw_opt_patched", False):
    _orig_run_command = _bass_utils.run_command

    def _run_command_ldw(cmd, *a, **kw):
        cmd = [c.replace("--enable-ldw-opt=false", "--enable-ldw-opt=true")
               if isinstance(c, str) else c for c in cmd]
        return _orig_run_command(cmd, *a, **kw)

    _bass_utils.run_command = _run_command_ldw
    _bass_utils._ldw_opt_patched = True

F32 = mybir.dt.float32
BF16 = mybir.dt.bfloat16
I32 = mybir.dt.int32
AF = mybir.ActivationFunctionType
ALU = mybir.AluOpType

B, S, D = 4, 8192, 1024
T = 4096
TC = 512
NT = T // TC
DI = 2048
DS = 64
HID = 3072
NH, DH, KM = 16, 64, 128
QS = 1.0 / 127.0
EPS = 1e-5
INV16PI = 1.0 / (16.0 * math.pi)
TWO_PI = 2.0 * math.pi
GROUPS = [[0, 1], [2, 3], [4, 5], [6, 7]]


class Sem:
    def __init__(self, handle, step):
        self.h = handle
        self.step = step
        self.n = 0

    def bump(self):
        self.n += self.step
        return self.n


class Builder:
    def __init__(self, upto=4, dbg=False, paranoid=0, rdbg=False,
                 attn_stop=99):
        self.attn_stop = attn_stop
        self.upto = upto
        self.dbg = dbg
        self.paranoid = paranoid
        self.rdbg = rdbg
        nc = bass.Bass("TRN2", target_bir_lowering=False,
                       detect_race_conditions=rdbg)
        self.nc = nc
        self.prog = {"t": [], "s": [], "v": [], "g": [], "y": []}
        self.es = ExitStack()

        di = {}

        def inp(name, shape, dt=F32):
            di[name] = nc.dram_tensor(name, list(shape), dt,
                                      kind="ExternalInput").ap()
        inp("xt", (D, T))
        inp("w_in", (128, 8 * 2 * DI)); inp("b_in_r", (128, 32))
        inp("w_dt", (128, 16 * D)); inp("b_dt_r", (128, 8))
        inp("w_xp", (128, 16 * DS)); inp("b_xp_r", (DS, 1))
        inp("w_out", (128, 16 * D)); inp("b_out_r", (128, 8))
        inp("a_log_r", (DS, 1)); inp("d_r", (128, 8))
        inp("proj", (DH, KM))
        for wn in ("wq_t", "wk_t", "wv_t", "wo_t"):
            inp(wn, (128, 8 * D))
        inp("bq_r", (128, 8)); inp("bk_r", (128, 8)); inp("bo_r", (128, 8))
        inp("bv_row", (1, D))
        inp("wg_t", (128, 8 * HID), I32); inp("wu_t", (128, 8 * HID), I32)
        inp("wd_t", (128, 24 * D), I32)
        inp("eye", (128, DS))
        inp("isfirst", (DS, 1)); inp("notfirst", (DS, 1))
        self.di = di
        self.yt = nc.dram_tensor("yt", [D, T], F32, kind="ExternalOutput").ap()

        self.agx_in = nc.dram_tensor("agx_in", [DS, 1], F32)
        self.agx_out = nc.dram_tensor("agx_out", [2 * DS, 1], F32)
        self.agy_in = nc.dram_tensor("agy_in", [DS, 1], F32)
        self.agy_out = nc.dram_tensor("agy_out", [2 * DS, 1], F32)
        self.agz_in = nc.dram_tensor("agz_in", [DS, 1], F32)
        self.agz_out = nc.dram_tensor("agz_out", [2 * DS, 1], F32)
        self.kv_in = nc.dram_tensor("kv_in", [128, 2048], F32)
        self.kv_out = nc.dram_tensor("kv_out", [128, 2048], F32)
        self.pdram = nc.dram_tensor("pdram", [128, 8 * 8 * TC], BF16)
        if getattr(self, "dbg", False):
            self.dbg_kv = nc.dram_tensor("dbg_kv", [128, 2048], F32,
                                         kind="ExternalOutput").ap()
            self.dbg_kp = nc.dram_tensor("dbg_kp", [128, NH * 256], F32,
                                         kind="ExternalOutput").ap()
            self.dbg_qph = nc.dram_tensor("dbg_qph", [128, 1024], F32,
                                          kind="ExternalOutput").ap()
            self.dbg_afm = nc.dram_tensor("dbg_afm", [128, 8 * TC], F32,
                                          kind="ExternalOutput").ap()
        self.sdram = nc.dram_tensor("sdram", [2, T], F32)

        ec = self.es.enter_context
        self._ec = ec
        self._eng_sems = {k: [] for k in ("t", "s", "v", "g", "y")}
        self._nsem = [0]

        def mksem(eng, step):
            s = Sem(ec(nc.semaphore("sm%d" % self._nsem[0])), step)
            self._nsem[0] += 1
            self._eng_sems[eng].append(s)
            return s
        self.mksem = mksem
        self.sy = mksem("y", 16)
        self.gw = mksem("g", 16)
        self.cc = mksem("g", 1)
        self.ts = mksem("t", 1)
        self.ss = mksem("s", 1)
        self.vs = mksem("v", 1)
        self.gs = mksem("g", 1)
        self.xls = mksem("g", 16)
        self.cst = mksem("y", 16)
        self.ccb = mksem("g", 16)
        self.srw = mksem("y", 16)
        self.sld = [mksem("y", 16), mksem("y", 16)]
        self.pst = [mksem("y", 16), mksem("y", 16)]
        self.pld = [mksem("y", 16), mksem("y", 16)]
        self.ost = [mksem("y", 16), mksem("y", 16)]

        def sb(name, p, f, dt):
            return ec(nc.sbuf_tensor(name, [p, f], dt))
        self.stream = sb("stream", 128, 8 * T, BF16)
        self.ln_bf = sb("ln_bf", 128, 2 * 8 * TC, BF16)
        self.sq_scr = sb("sq_scr", 128, 2 * TC, BF16)
        self.sq_st = sb("sq_st", 128, 2 * TC, BF16)
        self.cscr = sb("cscr", 128, TC, F32)
        self.mean_pr = sb("mean_pr", 1, 2 * TC, F32)
        self.rstd_pr = sb("rstd_pr", 1, 2 * TC, F32)
        self.rowA = sb("rowA", 1, TC, F32)
        self.rowB = sb("rowB", 1, TC, F32)
        self.b_in_s = sb("b_in_s", 128, 32, F32)
        self.b_dt_s = sb("b_dt_s", 128, 8, F32)
        self.nb_dt_s = sb("nb_dt_s", 128, 8, F32)
        self.b_xp_s = sb("b_xp_s", DS, 1, F32)
        self.b_out_s = sb("b_out_s", 128, 8, F32)
        self.a_log_s = sb("a_log_s", DS, 1, F32)
        self.aexp_s = sb("aexp_s", DS, 1, F32)
        self.d_s = sb("d_s", 128, 8, F32)
        self.bq_s = sb("bq_s", 128, 8, F32)
        self.bk_s = sb("bk_s", 128, 8, F32)
        self.bo_s = sb("bo_s", 128, 8, F32)
        self.bv_bf = sb("bv_bf", 1, D, BF16)
        self.projz = sb("projz", 128, 2 * KM, BF16)
        self.eye_f = sb("eye_f", 128, DS, F32)
        self.identf = sb("identf", 1, 1, F32)
        self.onesb = sb("onesb", 128, 1, BF16)
        self.ones64f = sb("ones64f", DS, 1, F32)
        self.onesbcf = sb("onesbcf", 1, 128, F32)
        self.ones64bc = sb("ones64bc", 1, DS, F32)
        self.ones1_bf = sb("ones1_bf", 1, 128, BF16)
        self.zeros64 = sb("zeros64", DS, 1, F32)
        self.ones64 = sb("ones64", DS, 1, F32)
        self.c_eps = sb("c_eps", 1, 1, F32)
        self.c_075 = sb("c_075", 128, 1, F32)
        self.c_hpi = sb("c_hpi", 128, 1, F32)
        self.c_negpi = sb("c_negpi", 128, 1, F32)
        self.isf_s = sb("isf_s", DS, 1, F32)
        self.nf_s = sb("nf_s", DS, 1, F32)
        self.recvX = sb("recvX", DS, 1, F32)
        self.recvY = sb("recvY", DS, 1, F32)
        self.recvZ = sb("recvZ", DS, 1, F32)
        self.cvec = sb("cvec", DS, 8, F32)
        self.wsl_a = sb("wsl_a", 128, 2 * 8 * 256, BF16)
        self.wsl_b = sb("wsl_b", 128, 2 * 16 * 128, BF16)
        self.nrm_scr = self.cscr

        self.pA = ec(nc.psum_tensor("pA", [128, TC], F32))
        self.pB = ec(nc.psum_tensor("pB", [128, TC], F32))
        self.pC = ec(nc.psum_tensor("pC", [128, TC], F32))
        self.pD = ec(nc.psum_tensor("pD", [128, TC], F32))
        self.pROW = ec(nc.psum_tensor("pROW", [128, TC], F32))
        self.pBC1 = ec(nc.psum_tensor("pBC1", [128, TC], F32))
        self.pBC2 = ec(nc.psum_tensor("pBC2", [128, TC], F32))
        self.pBC3 = ec(nc.psum_tensor("pBC3", [128, TC], F32))
        self.kv_banks = [self.pC, self.pD, self.pROW, self.pBC3]

        if self.rdbg:
            self.esem = {k: Sem(ec(nc.semaphore("esem_" + k)), 1)
                         for k in ("t", "s", "v", "g", "y")}
        self.pguard = {}
        self._last_nrm = 0
        self.lnbf_used = [0, 0]
        self._pnames = {id(p): n for p, n in
                        [(self.pA, "pA"), (self.pB, "pB"), (self.pC, "pC"),
                         (self.pD, "pD"), (self.pROW, "pROW"),
                         (self.pBC1, "pBC1"), (self.pBC2, "pBC2"),
                         (self.pBC3, "pBC3")]}

    def barrier(self, level=1, tag=None):
        import os
        tags = set(os.environ.get("PARANOID_TAGS", "").split(","))
        if self.paranoid < level and (tag is None or tag not in tags):
            return
        sems = [self.ts, self.ss, self.vs, self.gw, self.sy, self.cc, self.gs]
        for eng in ("t", "s", "v", "g", "y"):
            for sm in sems:
                self.wait(eng, sm, sm.n)

    def wait(self, eng, sem, val):
        if val and val > 0:
            self.prog[eng].append(lambda e, h=sem.h, v=val: e.wait_ge(h, v))

    def _rdbg_prewait(self, eng):
        es = self.esem[eng]
        if es.n:
            self.prog[eng].append(lambda e, h=es.h, v=es.n: e.wait_ge(h, v))
        for sm in self._eng_sems[eng]:
            if sm.n:
                self.prog[eng].append(
                    lambda e, h=sm.h, v=sm.n: e.wait_ge(h, v))

    def emit(self, eng, fn, has_inc=False):
        if self.rdbg:
            self._rdbg_prewait(eng)
            if has_inc:
                self.prog[eng].append(fn)
            else:
                es = self.esem[eng]
                self.prog[eng].append(
                    lambda e, f=fn, h=es.h: f(e).then_inc(h, 1))
                es.n += 1
        else:
            self.prog[eng].append(fn)

    def inc(self, eng, sem, fn):
        if self.rdbg and sem.step == 16 or (self.rdbg and sem is self.cc):
            self.wait(eng, sem, sem.n)
        self.emit(eng, lambda e, f=fn, h=sem.h, st=sem.step:
                  f(e).then_inc(h, st), has_inc=True)
        return sem.bump()

    def st_ch(self, mc, j=None, cols=TC):
        if j is None:
            return self.stream[:, mc * T:(mc + 1) * T]
        c0 = mc * T + j * TC
        return self.stream[:, c0:c0 + cols]

    def waitv(self, eng, sv):
        sem, val = sv
        self.wait(eng, sem, val)

    def pwait(self, ps):
        g = self.pguard.get(id(ps))
        if g:
            eng, val = g
            sem = {"s": self.ss, "v": self.vs}[eng]
            self.wait("t", sem, val)

    def pset(self, ps, eng, val):
        self.pguard[id(ps)] = (eng, val)

    def mm_group(self, ps, pairs, arm=True, close=True, guard=True):
        if guard:
            self.pwait(ps)
        n = len(pairs)
        last = 0
        for i, (stat, mov) in enumerate(pairs):
            st_f = arm and (i == 0)
            sp_f = close and (i == n - 1)
            if i < n - 1:
                self.emit("t", lambda e, p=ps, s=stat, m=mov, a=st_f, b=sp_f:
                          e.matmul(p[:], s, m, start=a, stop=b,
                                   skip_group_check=True))
            else:
                last = self.inc("t", self.ts,
                                lambda e, p=ps, s=stat, m=mov, a=st_f, b=sp_f:
                                e.matmul(p[:], s, m, start=a, stop=b,
                                         skip_group_check=True))
        return last

    def emit_setup(self):
        di = self.di
        for name, dst in [("b_in_r", self.b_in_s), ("b_dt_r", self.b_dt_s),
                          ("b_xp_r", self.b_xp_s), ("b_out_r", self.b_out_s),
                          ("a_log_r", self.a_log_s), ("d_r", self.d_s),
                          ("bq_r", self.bq_s), ("bk_r", self.bk_s),
                          ("bo_r", self.bo_s), ("eye", self.eye_f),
                          ("isfirst", self.isf_s), ("notfirst", self.nf_s)]:
            self.inc("y", self.cst,
                     lambda e, a=di[name], d=dst: e.dma_start(out=d[:], in_=a))
        self.v_bias = self.cst.n
        self.bias_sem = self.cst
        self.inc("g", self.gs, lambda e: e.memset(self.projz[:], 0.0))
        self.inc("g", self.gw, lambda e: e.dma_start(out=self.projz[0:DH, 0:KM],
                                                     in_=di["proj"]))
        self.inc("g", self.gw, lambda e: e.dma_start(
            out=self.projz[DH:128, KM:2 * KM], in_=di["proj"]))
        self.inc("g", self.gw, lambda e: e.dma_start(out=self.bv_bf[:],
                                                     in_=di["bv_row"]))
        self.v_setup_gw = self.gw.n
        for mc in range(8):
            self.inc("g", self.xls, lambda e, mc=mc: e.dma_start(
                out=self.st_ch(mc), in_=di["xt"][mc * 128:(mc + 1) * 128, :]))
        self.v_xload = self.xls.n
        for dst, val in [(self.onesb, 1.0), (self.ones64f, 1.0),
                         (self.onesbcf, 1.0), (self.ones64bc, 1.0 / 1024.0),
                         (self.ones1_bf, 1.0), (self.zeros64, 0.0),
                         (self.ones64, 1.0),
                         (self.identf, 1.0), (self.c_eps, EPS),
                         (self.c_075, 0.75), (self.c_negpi, -math.pi),
                         (self.c_hpi, math.pi / 2)]:
            self.inc("g", self.gs, lambda e, d=dst, c=val: e.memset(d[:], c))
        self.v_memset = self.gs.n
        self.wait("s", self.cst, self.v_bias)
        self.inc("s", self.ss, lambda e: e.mul(out=self.nb_dt_s[:],
                                               in_=self.b_dt_s[:], mul=-1.0))
        self.inc("s", self.ss, lambda e: e.activation(
            out=self.aexp_s[:], in_=self.a_log_s[:], func=AF.Exp))
        self.v_consts = self.ss.n
        self._first_stats = True

    def _stats_one_tile(self, j, dst_mean, dst_rstd, rowbank=None):
        rb = rowbank if rowbank is not None else self.pROW
        if self._first_stats:
            self.wait("t", self.gs, self.v_memset)
            self.wait("s", self.xls, self.v_xload)
            self._first_stats = False
        self.pwait(rb)
        self.wait("s", self.vs, self.vs.n)
        sq_done = []
        for mc in range(8):
            par = mc % 2
            if mc >= 2:
                self.wait("s", self.ts, sq_done[mc - 2])
            vsq = self.inc("s", self.ss, lambda e, mc=mc, j=j, par=par:
                           e.activation(
                               out=self.sq_st[:, par * TC:(par + 1) * TC],
                               in_=self.st_ch(mc, j), func=AF.Square))
            self.wait("t", self.ss, vsq)
            self.emit("t", lambda e, mc=mc, j=j, rb=rb: e.matmul(
                rb[0:1, :], self.onesb[:], self.st_ch(mc, j),
                start=(mc == 0), stop=(mc == 7), skip_group_check=True))
            vmm = self.inc("t", self.ts, lambda e, par=par, mc=mc, rb=rb:
                           e.matmul(
                               rb[32:33, :], self.onesb[:],
                               self.sq_st[:, par * TC:(par + 1) * TC],
                               start=(mc == 0), stop=(mc == 7),
                               skip_group_check=True))
            sq_done.append(vmm)
        self.wait("s", self.ts, sq_done[-1])
        self.inc("s", self.ss, lambda e, d=dst_mean, rb=rb: e.activation(
            out=d, in_=rb[0:1, :], func=AF.Copy, scale=1.0 / 1024.0))
        vr = self.inc("s", self.ss, lambda e, rb=rb: e.activation(
            out=self.rowA[:], in_=rb[32:33, :], func=AF.Copy,
            scale=1.0 / 1024.0))
        self.pset(rb, "s", vr)
        vq2 = self.inc("s", self.ss, lambda e, d=dst_mean: e.activation(
            out=self.rowB[:], in_=d, func=AF.Square))
        self.wait("v", self.ss, vq2)
        vv = self.inc("v", self.vs, lambda e: e.tensor_tensor(
            self.rowA[:], self.rowA[:], self.rowB[:], ALU.subtract))
        self.wait("s", self.vs, vv)
        self.inc("s", self.ss, lambda e: e.activation(
            out=self.rowB[:], in_=self.rowA[:], func=AF.Ln, bias=self.c_eps[:]))
        return self.inc("s", self.ss, lambda e, d=dst_rstd: e.activation(
            out=d, in_=self.rowB[:], func=AF.Exp, scale=-0.5))

    def emit_stats_pair(self, j0, j1=None, rowbank=None):
        v = self._stats_one_tile(j0, self.mean_pr[:, 0:TC],
                                 self.rstd_pr[:, 0:TC], rowbank)
        if j1 is not None:
            v = self._stats_one_tile(j1, self.mean_pr[:, TC:2 * TC],
                                     self.rstd_pr[:, TC:2 * TC], rowbank)
        return v

    def emit_normalize_tile(self, j, jp, rows_sem, rows_val):
        self.wait("t", rows_sem, rows_val)
        self.wait("t", self.vs, self._last_nrm)
        self.wait("t", self.ts, self.lnbf_used[jp])
        self.pwait(self.pBC1)
        self.pwait(self.pBC2)
        self.emit("t", lambda e, jp=jp: e.matmul(
            self.pBC1[:], self.onesbcf[:],
            self.mean_pr[:, jp * TC:(jp + 1) * TC],
            start=True, stop=True, skip_group_check=True))
        vbc = self.inc("t", self.ts, lambda e, jp=jp: e.matmul(
            self.pBC2[:], self.onesbcf[:],
            self.rstd_pr[:, jp * TC:(jp + 1) * TC],
            start=True, stop=True, skip_group_check=True))
        self.wait("v", self.ts, vbc)
        x3 = self.stream[:].rearrange("p (m t) -> p m t", m=8)[
            :, :, j * TC:(j + 1) * TC]
        l3 = self.ln_bf[:, jp * 8 * TC:(jp + 1) * 8 * TC].rearrange(
            "p (m t) -> p m t", m=8)
        self.emit("v", lambda e, x3=x3, l3=l3: e.tensor_tensor(
            l3, x3, self.pBC1[:].unsqueeze(1).broadcast_to([128, 8, TC]),
            ALU.subtract))
        last = self.inc("v", self.vs, lambda e, l3=l3: e.tensor_tensor(
            l3, l3, self.pBC2[:].unsqueeze(1).broadcast_to([128, 8, TC]),
            ALU.mult))
        self._last_nrm = last
        self.pset(self.pBC1, "v", last)
        self.pset(self.pBC2, "v", last)
        return last

    def slab_loader(self, buf, slab_elems, dram_ap, kc, slab_cols,
                    double=True):
        sems = [self.mksem("g", 16), self.mksem("g", 16)]
        state = {"used": [self.ts.n, self.ts.n], "seq": 0}
        w = kc * slab_cols

        def load(slab_no):
            par = (state["seq"] % 2) if double else 0
            state["seq"] += 1
            if state["used"][par]:
                self.wait("g", self.ts, state["used"][par])
            c0 = slab_no * w
            out_ap = buf[:, par * slab_elems:par * slab_elems + w]
            v = self.inc("g", sems[par],
                         lambda e, o=out_ap, a=dram_ap[:, c0:c0 + w]:
                         e.dma_start(out=o, in_=a))
            return par, (sems[par], v)

        def used(par, ts_val):
            state["used"][par] = ts_val
        return load, used

    def stat_ap(self, buf, par, slab_elems, kc_cols, kc, off, width=128):
        c0 = par * slab_elems + kc * kc_cols + off
        return buf[:, c0:c0 + width]

    def emit_mamba(self):
        di = self.di
        sc = ExitStack()
        xi = sc.enter_context(self.nc.sbuf_tensor("xi", [128, 2 * 16 * TC], BF16))
        sgm = sc.enter_context(self.nc.sbuf_tensor("sgm", [128, 8 * TC], BF16))
        sloc = sc.enter_context(self.nc.sbuf_tensor("sloc", [DS, NT * TC], F32))
        apfx = sc.enter_context(self.nc.sbuf_tensor("apfx", [DS, NT * TC], BF16))
        a_t = sc.enter_context(self.nc.sbuf_tensor("a_t", [DS, TC], F32))
        bx_t = sc.enter_context(self.nc.sbuf_tensor("bx_t", [DS, TC], F32))
        scorr = sc.enter_context(self.nc.sbuf_tensor("scorr", [DS, TC], F32))
        pstage = sc.enter_context(self.nc.sbuf_tensor("pstage", [128, 2 * TC], BF16))
        self.wsl_c = sc.enter_context(
            self.nc.sbuf_tensor("wsl_cm", [128, 2 * 16 * 128], BF16))
        self.wsl_d = sc.enter_context(
            self.nc.sbuf_tensor("wsl_d", [128, 16 * 64], BF16))
        self.zscan = sc.enter_context(
            self.nc.sbuf_tensor("zscan", [DS, TC], F32))
        vzs = self.inc("g", self.gs, lambda e: e.memset(self.zscan[:], 0.0))

        ld_in, used_in = self.slab_loader(self.wsl_a, 8 * 256, di["w_in"], 8, 256)
        ld_dt, used_dt = self.slab_loader(self.wsl_b, 16 * 128, di["w_dt"], 16, 128)
        ld_xp, _ = self.slab_loader(self.wsl_d, 16 * 64, di["w_xp"], 16, 64,
                                    double=False)
        ld_out, used_out = self.slab_loader(self.wsl_c, 16 * 128, di["w_out"], 16, 128)
        scan_done = [0] * NT
        pst_free = [None, None]
        xp_loaded = [None]

        vst_pre = None
        for pair in range(4):
            j0, j1 = 2 * pair, 2 * pair + 1
            vst = vst_pre if vst_pre is not None \
                else self.emit_stats_pair(j0, j1)
            vln = [self.emit_normalize_tile(j0, 0, self.ss, vst),
                   self.emit_normalize_tile(j1, 1, self.ss, vst)]
            for jp, j in ((0, j0), (1, j1)):
                for mc in range(8):
                    vln[jp] = self.inc("v", self.vs, lambda e, mc=mc, j=j, jp=jp:
                                       e.scalar_tensor_tensor(
                                           self.st_ch(mc, j),
                                           self.ln_bf[:, (jp * 8 + mc) * TC:
                                                      (jp * 8 + mc + 1) * TC],
                                           self.d_s[:, mc:mc + 1],
                                           self.st_ch(mc, j),
                                           ALU.mult, ALU.add))

            self.barrier(1)
            for sl in range(8):
                par, vld = ld_in(sl)
                self.waitv("t", vld)
                for half in range(2):
                    mc = sl * 2 + half
                    for jp in (0, 1):
                        self.wait("t", self.vs, vln[jp])
                        ps = (self.pA, self.pB)[(mc * 2 + jp) % 2]
                        prs = [(self.stat_ap(self.wsl_a, par, 8 * 256, 256,
                                             kc, half * 128),
                                self.ln_bf[:, (jp * 8 + kc) * TC:
                                           (jp * 8 + kc + 1) * TC])
                               for kc in range(8)]
                        lm = self.mm_group(ps, prs)
                        self.wait("s", self.ts, lm)
                        vev = self.inc("s", self.ss,
                                       lambda e, p=ps, mc=mc, jp=jp:
                                       e.activation(
                                           out=xi[:, (jp * 16 + mc) * TC:
                                                  (jp * 16 + mc + 1) * TC],
                                           in_=p[:], func=AF.Silu,
                                           bias=self.b_in_s[:, mc:mc + 1]))
                        self.pset(ps, "s", vev)
                used_in(par, self.ts.n)
                self.barrier(2, "A")

            self.barrier(1)
            for jp, j in enumerate((j0, j1)):
                for md in range(8):
                    par, vld = ld_dt(md)
                    self.waitv("t", vld)
                    ps = (self.pA, self.pB)[md % 2]
                    prs = [(self.stat_ap(self.wsl_b, par, 16 * 128, 128, kc, 0),
                            xi[:, (jp * 16 + kc) * TC:(jp * 16 + kc + 1) * TC])
                           for kc in range(16)]
                    lm = self.mm_group(ps, prs)
                    self.wait("s", self.ts, lm)
                    vsg = self.inc("s", self.ss, lambda e, p=ps, md=md:
                                   e.activation(
                                       out=sgm[:, md * TC:(md + 1) * TC],
                                       in_=p[:], func=AF.Sigmoid, scale=-1.0,
                                       bias=self.nb_dt_s[:, md:md + 1]))
                    self.pset(ps, "s", vsg)
                    used_dt(par, self.ts.n)
                    self.barrier(2, "B")
                self.pwait(self.pROW)
                vmd = 0
                for md in range(8):
                    vl = self.inc("s", self.ss, lambda e, md=md: e.activation(
                        out=sgm[:, md * TC:(md + 1) * TC],
                        in_=sgm[:, md * TC:(md + 1) * TC], func=AF.Ln))
                    self.wait("t", self.ss, vl)
                    if md < 7:
                        self.emit("t", lambda e, md=md: e.matmul(
                            self.pROW[64:65, :], self.onesb[:],
                            sgm[:, md * TC:(md + 1) * TC],
                            start=(md == 0), stop=False,
                            skip_group_check=True))
                    else:
                        vmd = self.inc("t", self.ts, lambda e: e.matmul(
                            self.pROW[64:65, :], self.onesb[:],
                            sgm[:, 7 * TC:8 * TC],
                            start=False, stop=True, skip_group_check=True))
                self.wait("s", self.ts, vmd)
                vrow = self.inc("s", self.ss, lambda e: e.activation(
                    out=self.rowA[:], in_=self.pROW[64:65, :], func=AF.Copy))
                self.pset(self.pROW, "s", vrow)
                self.pwait(self.pBC3)
                self.wait("t", self.ss, vrow)
                vbc = self.inc("t", self.ts, lambda e: e.matmul(
                    self.pBC3[0:DS, :], self.ones64bc[:], self.rowA[:],
                    start=True, stop=True, skip_group_check=True))
                self.wait("s", self.ts, vbc)
                self.wait("s", self.ss, self.v_consts)
                va = self.inc("s", self.ss, lambda e, a_t=a_t: e.activation(
                    out=a_t[:], in_=self.pBC3[0:DS, :], func=AF.Exp,
                    scale=self.aexp_s[:]))
                self.pset(self.pBC3, "s", va)
                if xp_loaded[0] is None:
                    xp_loaded[0] = ld_xp(0)
                    self.waitv("t", xp_loaded[0][1])
                self.pwait(self.pBC3)
                lm = 0
                for kc in range(16):
                    statp = self.wsl_d[:, kc * 64:(kc + 1) * 64]
                    mov = xi[:, (jp * 16 + kc) * TC:(jp * 16 + kc + 1) * TC]
                    if kc < 15:
                        self.emit("t", lambda e, s=statp, m=mov, a=(kc == 0):
                                  e.matmul(self.pBC3[0:DS, :], s, m, start=a,
                                           stop=False, skip_group_check=True))
                    else:
                        lm = self.inc("t", self.ts, lambda e, s=statp, m=mov:
                                      e.matmul(self.pBC3[0:DS, :], s, m,
                                               start=False, stop=True,
                                               skip_group_check=True))
                self.wait("s", self.ts, lm)
                vbx = self.inc("s", self.ss, lambda e, bx_t=bx_t: e.activation(
                    out=bx_t[:], in_=self.pBC3[0:DS, :], func=AF.Identity,
                    bias=self.b_xp_s[:]))
                self.pset(self.pBC3, "s", vbx)
                self.wait("v", self.ss, max(va, vbx))
                self.wait("v", self.gs, vzs)
                if j % 4 == 0:
                    init_s, init_a = self.zeros64[:, 0:1], self.ones64[:, 0:1]
                else:
                    init_s = sloc[:, j * TC - 1:j * TC]
                    init_a = apfx[:, j * TC - 1:j * TC]
                self.emit("v", lambda e, j=j, i=init_s, a_t=a_t, bx_t=bx_t,
                          sloc=sloc: e.tensor_tensor_scan(
                              sloc[:, j * TC:(j + 1) * TC], a_t[:], bx_t[:],
                              i, ALU.mult, ALU.add))
                vsc = self.inc("v", self.vs, lambda e, j=j, i=init_a, a_t=a_t,
                               apfx=apfx: e.tensor_tensor_scan(
                                   apfx[:, j * TC:(j + 1) * TC], a_t[:],
                                   self.zscan[:], i, ALU.mult, ALU.add))
                scan_done[j] = vsc
                self.wait("s", self.vs, vsc)
                self.barrier(2, "C")

            self.barrier(1)
            for sl in range(8, 16):
                par, vld = ld_in(sl)
                self.waitv("t", vld)
                for half in range(2):
                    mc = (sl - 8) * 2 + half
                    for jp in (0, 1):
                        ps = (self.pA, self.pB)[(mc * 2 + jp) % 2]
                        prs = [(self.stat_ap(self.wsl_a, par, 8 * 256, 256,
                                             kc, half * 128),
                                self.ln_bf[:, (jp * 8 + kc) * TC:
                                           (jp * 8 + kc + 1) * TC])
                               for kc in range(8)]
                        lm = self.mm_group(ps, prs)
                        self.wait("s", self.ts, lm)
                        self.wait("s", self.vs, self.vs.n)
                        vev = self.inc("s", self.ss, lambda e, p=ps, mc=mc:
                                       e.activation(
                                           out=self.sq_scr[:, 0:TC], in_=p[:],
                                           func=AF.Silu,
                                           bias=self.b_in_s[:, 16 + mc:17 + mc]))
                        self.pset(ps, "s", vev)
                        self.wait("v", self.ss, vev)
                        vv = self.inc("v", self.vs, lambda e, mc=mc, jp=jp:
                                      e.tensor_tensor(
                                          xi[:, (jp * 16 + mc) * TC:
                                             (jp * 16 + mc + 1) * TC],
                                          xi[:, (jp * 16 + mc) * TC:
                                             (jp * 16 + mc + 1) * TC],
                                          self.sq_scr[:, 0:TC], ALU.mult))
                        self.wait("s", self.vs, vv)
                used_in(par, self.ts.n)
                self.barrier(2, "D")

            self.barrier(1)
            vv_all = self.vs.n
            for mo in range(8):
                par, vld = ld_out(mo)
                self.waitv("t", vld)
                self.wait("t", self.vs, vv_all)
                for jp, j in enumerate((j0, j1)):
                    jg = j
                    op = (mo * 2 + jp) % 2
                    ps = (self.pA, self.pB)[op]
                    prs = [(self.stat_ap(self.wsl_c, par, 16 * 128, 128, kc, 0),
                            xi[:, (jp * 16 + kc) * TC:(jp * 16 + kc + 1) * TC])
                           for kc in range(16)]
                    lm = self.mm_group(ps, prs)
                    self.wait("s", self.ts, lm)
                    if pst_free[op]:
                        self.waitv("s", pst_free[op])
                    vev = self.inc("s", self.ss, lambda e, p=ps, op=op:
                                   e.activation(
                                       out=pstage[:, op * TC:(op + 1) * TC],
                                       in_=p[:], func=AF.Copy))
                    self.pset(ps, "s", vev)
                    self.wait("y", self.ss, vev)
                    vstore = self.inc("y", self.pst[op],
                                      lambda e, jg=jg, mo=mo, op=op:
                                      e.dma_start(
                                          out=self.pdram.ap()[:, (jg * 8 + mo) * TC:
                                                              (jg * 8 + mo + 1) * TC],
                                          in_=pstage[:, op * TC:(op + 1) * TC]))
                    pst_free[op] = (self.pst[op], vstore)
                used_out(par, self.ts.n)
                self.barrier(2, "E")
            self.lnbf_used[0] = self.ts.n
            self.lnbf_used[1] = self.ts.n
            if pair < 3:
                vst_pre = self.emit_stats_pair(j0 + 2, j1 + 2)

        p_stored_val = (self.pst[0].n, self.pst[1].n)
        self.barrier(1)
        self.emit_carry_final(sloc, apfx, scan_done)
        self.barrier(1)
        self.emit_compose_group(0, 2, sloc, apfx, scorr, pstage,
                                pst_free, p_stored_val)
        self.emit_compose_group(1, 3, sloc, apfx, scorr, pstage,
                                pst_free, p_stored_val)
        sc.close()

    def _stage_carry(self, src_ap, dram_in, dram_out, recv_dst, recv_hi):
        v1 = self.inc("g", self.ccb, lambda e, s=src_ap, d=dram_in:
                      e.dma_start(out=d.ap(), in_=s))
        self.wait("g", self.ccb, v1)
        vcc = self.inc("g", self.cc, lambda e, i=dram_in, o=dram_out:
                       e.collective_compute("AllGather", ALU.bypass,
                                            replica_groups=GROUPS,
                                            ins=[i.ap().opt()],
                                            outs=[o.ap().opt()]))
        self.wait("g", self.cc, vcc)
        sl = slice(DS, 2 * DS) if recv_hi else slice(0, DS)
        return self.inc("g", self.ccb, lambda e, o=dram_out, r=recv_dst, sl=sl:
                        e.dma_start(out=r[:], in_=o.ap()[sl, :]))

    def emit_carry_final(self, sloc, apfx, scan_done):
        self.wait("v", self.vs, scan_done[7])
        self.wait("v", self.cst, self.v_bias)
        vtf = self.inc("v", self.vs, lambda e, sloc=sloc, apfx=apfx:
                       e.scalar_tensor_tensor(
                           self.cvec[:, 1:2], apfx[:, 8 * TC - 1:8 * TC],
                           sloc[:, 4 * TC - 1:4 * TC],
                           sloc[:, 8 * TC - 1:8 * TC],
                           ALU.mult, ALU.add))
        self.wait("g", self.vs, vtf)
        vrx = self._stage_carry(self.cvec[:, 1:2], self.agx_in, self.agx_out,
                                self.recvX, False)
        self.wait("v", self.ccb, vrx)
        self.emit("v", lambda e: e.tensor_scalar(
            self.cvec[:, 2:3], self.recvX[:], self.nf_s[:], None, ALU.mult))
        self.inc("v", self.vs, lambda e, sloc=sloc, apfx=apfx:
                 e.scalar_tensor_tensor(
                     self.cvec[:, 3:4], apfx[:, 4 * TC - 1:4 * TC],
                     self.cvec[:, 2:3], sloc[:, 4 * TC - 1:4 * TC],
                     ALU.mult, ALU.add))

    def emit_compose_group(self, grp, ccol, sloc, apfx, scorr, pstage,
                           pst_free, p_stored_val):
        cv = self.vs.n
        for jj in range(4):
            j = grp * 4 + jj
            jg = j
            self.wait("v", self.vs, cv)
            vsc = self.inc("v", self.vs, lambda e, j=j, ccol=ccol, sloc=sloc,
                           apfx=apfx, scorr=scorr:
                           e.scalar_tensor_tensor(
                               scorr[:], apfx[:, j * TC:(j + 1) * TC],
                               self.cvec[:, ccol:ccol + 1],
                               sloc[:, j * TC:(j + 1) * TC],
                               ALU.mult, ALU.add))
            self.pwait(self.pROW)
            self.wait("t", self.vs, vsc)
            vmm = self.inc("t", self.ts, lambda e, scorr=scorr: e.matmul(
                self.pROW[64:65, :], self.ones64f[:], scorr[:],
                start=True, stop=True, skip_group_check=True))
            self.wait("s", self.ts, vmm)
            vrow = self.inc("s", self.ss, lambda e: e.activation(
                out=self.rowB[:], in_=self.pROW[64:65, :], func=AF.Copy))
            self.pset(self.pROW, "s", vrow)
            self.pwait(self.pBC3)
            self.wait("t", self.ss, vrow)
            vbc = self.inc("t", self.ts, lambda e: e.matmul(
                self.pBC3[:], self.onesbcf[:], self.rowB[:],
                start=True, stop=True, skip_group_check=True))
            self.wait("v", self.ts, vbc)
            vcm = 0
            for mc in range(8):
                op = mc % 2
                self.wait("y", self.pst[0], p_stored_val[0])
                self.wait("y", self.pst[1], p_stored_val[1])
                if pst_free[op]:
                    self.waitv("y", pst_free[op])
                vload = self.inc("y", self.pld[op], lambda e, jg=jg, mc=mc,
                                 op=op, pstage=pstage:
                                 e.dma_start(
                                     out=pstage[:, op * TC:(op + 1) * TC],
                                     in_=self.pdram.ap()[:, (jg * 8 + mc) * TC:
                                                         (jg * 8 + mc + 1) * TC]))
                self.wait("v", self.pld[op], vload)
                self.emit("v", lambda e, op=op, pstage=pstage: e.tensor_tensor(
                    self.cscr[:], pstage[:, op * TC:(op + 1) * TC],
                    self.pBC3[:], ALU.mult))
                vcm = self.inc("v", self.vs, lambda e, mc=mc, j=j:
                               e.scalar_tensor_tensor(
                                   self.st_ch(mc, j), self.cscr[:],
                                   self.b_out_s[:, mc:mc + 1],
                                   self.st_ch(mc, j), ALU.add, ALU.add))
                pst_free[op] = (self.vs, vcm)
            self.pset(self.pBC3, "v", vcm)

    def emit_ffn(self, final=False):
        di = self.di
        sc = ExitStack()
        hbuf = sc.enter_context(self.nc.sbuf_tensor(
            "hbuf_f" if final else "hbuf", [128, 2 * 24 * TC], BF16))
        if final:
            ostage = sc.enter_context(self.nc.sbuf_tensor(
                "ostage", [128, 2 * TC], F32))
        self.wsl_c = sc.enter_context(self.nc.sbuf_tensor(
            "wsl_cf%d" % (1 if final else 0), [128, 2 * 24 * 128], BF16))
        ld_g, used_g = self.slab_loader(self.wsl_a, 8 * 128, di["wg_t"], 8, 128)
        ld_u, used_u = self.slab_loader(self.wsl_b, 8 * 128, di["wu_t"], 8, 128)
        ld_d, used_d = self.slab_loader(self.wsl_c, 24 * 128, di["wd_t"], 24, 128)
        ost_free = [0, 0]
        vst_pre = None
        for pair in range(4):
            j0, j1 = 2 * pair, 2 * pair + 1
            vst = vst_pre if vst_pre is not None \
                else self.emit_stats_pair(j0, j1)
            vln = [self.emit_normalize_tile(j0, 0, self.ss, vst),
                   self.emit_normalize_tile(j1, 1, self.ss, vst)]
            for mh in range(24):
                parg, vldg = ld_g(mh)
                paru, vldu = ld_u(mh)
                self.waitv("t", vldg); self.waitv("t", vldu)
                for jp in (0, 1):
                    self.wait("t", self.vs, vln[jp])
                    prs = [(self.stat_ap(self.wsl_a, parg, 8 * 128, 128, kc, 0),
                            self.ln_bf[:, (jp * 8 + kc) * TC:
                                       (jp * 8 + kc + 1) * TC])
                           for kc in range(8)]
                    lg = self.mm_group(self.pA, prs)
                    prs = [(self.stat_ap(self.wsl_b, paru, 8 * 128, 128, kc, 0),
                            self.ln_bf[:, (jp * 8 + kc) * TC:
                                       (jp * 8 + kc + 1) * TC])
                           for kc in range(8)]
                    lu = self.mm_group(self.pB, prs)
                    self.wait("s", self.ts, lg)
                    self.wait("s", self.vs, self.vs.n)
                    vsg = self.inc("s", self.ss, lambda e: e.activation(
                        out=self.sq_scr[:, 0:TC], in_=self.pA[:],
                        func=AF.Silu, scale=QS))
                    self.pset(self.pA, "s", vsg)
                    self.wait("v", self.ss, vsg)
                    self.wait("v", self.ts, lu)
                    vh = self.inc("v", self.vs, lambda e, mh=mh, jp=jp,
                                  hbuf=hbuf: e.tensor_tensor(
                                      hbuf[:, (jp * 24 + mh) * TC:
                                           (jp * 24 + mh + 1) * TC],
                                      self.sq_scr[:, 0:TC], self.pB[:],
                                      ALU.mult))
                    self.pset(self.pB, "v", vh)
                    self.wait("s", self.vs, vh)
                used_g(parg, self.ts.n)
                used_u(paru, self.ts.n)
            vh_all = self.vs.n
            for md in range(8):
                par, vld = ld_d(md)
                self.waitv("t", vld)
                self.wait("t", self.vs, vh_all)
                for jp, j in enumerate((j0, j1)):
                    ps = (self.pA, self.pB)[jp]
                    prs = [(self.stat_ap(self.wsl_c, par, 24 * 128, 128, kc, 0),
                            hbuf[:, (jp * 24 + kc) * TC:(jp * 24 + kc + 1) * TC])
                           for kc in range(24)]
                    lm = self.mm_group(ps, prs)
                    self.wait("v", self.ts, lm)
                    if not final:
                        vres = self.inc("v", self.vs,
                                        lambda e, p=ps, md=md, j=j:
                                        e.scalar_tensor_tensor(
                                            self.st_ch(md, j), p[:], QS * QS,
                                            self.st_ch(md, j), ALU.mult,
                                            ALU.add))
                    else:
                        op = (md * 2 + jp) % 2
                        if ost_free[op]:
                            self.wait("v", self.ost[op], ost_free[op])
                        vres = self.inc("v", self.vs,
                                        lambda e, p=ps, md=md, j=j, op=op,
                                        ostage=ostage:
                                        e.scalar_tensor_tensor(
                                            ostage[:, op * TC:(op + 1) * TC],
                                            p[:], QS * QS, self.st_ch(md, j),
                                            ALU.mult, ALU.add))
                        self.wait("y", self.vs, vres)
                        ost_free[op] = self.inc(
                            "y", self.ost[op], lambda e, md=md, j=j, op=op,
                            ostage=ostage: e.dma_start(
                                out=self.yt[md * 128:(md + 1) * 128,
                                            j * TC:(j + 1) * TC],
                                in_=ostage[:, op * TC:(op + 1) * TC]))
                    self.pset(ps, "v", vres)
                used_d(par, self.ts.n)
            self.lnbf_used[0] = self.ts.n
            self.lnbf_used[1] = self.ts.n
            if pair < 3:
                vst_pre = self.emit_stats_pair(j0 + 2, j1 + 2)
        sc.close()

    def emit_attn(self):
        di = self.di
        self.sc_attn = ExitStack()
        NPAR = 3
        self.wcos = [self.sc_attn.enter_context(
            self.nc.sbuf_tensor("wcos%d" % i, [128, 2 * TC], F32))
            for i in range(NPAR)]
        self.wsin = [self.sc_attn.enter_context(
            self.nc.sbuf_tensor("wsin%d" % i, [128, 2 * TC], F32))
            for i in range(NPAR)]
        self.ibuf = self.sc_attn.enter_context(
            self.nc.sbuf_tensor("ibuf", [128, 2 * TC], I32))
        self.fs_vdone = [0] * NPAR
        self.fs_v2 = [0] * NPAR
        self.fs_npar = NPAR
        self.fs_par = 0
        kvS = self.sc_attn.enter_context(
            self.nc.sbuf_tensor("kvS", [128, NH * 2 * DH], BF16))
        sc1 = ExitStack()
        kfm = sc1.enter_context(self.nc.sbuf_tensor("kfm", [128, 2 * 8 * TC], BF16))
        vtm = sc1.enter_context(self.nc.sbuf_tensor("vtm", [128, 2 * 4 * D], BF16))
        kp = sc1.enter_context(self.nc.sbuf_tensor("kp", [128, NH * 256], BF16))
        wvsl = sc1.enter_context(self.nc.sbuf_tensor("wvsl", [128, 2 * 8 * TC], BF16))

        ld_k, used_k = self.slab_loader(self.wsl_b, 8 * 128, di["wk_t"], 8, 128)
        ld_v, used_v = self.slab_loader(wvsl, 8 * TC, di["wv_t"], 8, TC)
        kv_armed = set()
        kp_used = {0: 0, 1: 0}
        vkv = 0

        def drain_kv(pd):
            nonlocal vkv
            ppair, pjp, ptc, pu, ppar = pd
            base = pu * 8 * 256
            cos_out = kp[:, base:base + 8 * 256].rearrange(
                "p (h m) -> p h m", h=8)[:, :, 0:128]
            sin_out = kp[:, base:base + 8 * 256].rearrange(
                "p (h m) -> p h m", h=8)[:, :, 128:256]
            vb = self.foursin_b(
                cos_out, sin_out, 2 * TC, ppar,
                out_guard=(self.ts, kp_used[pu]))
            if self.attn_stop < 14:
                return
            self.wait("t", self.vs, vb)
            last_tc = (ppair == 3 and pjp == 1 and ptc == 3)
            for hh in range(8):
                h = pu * 8 + hh
                bank = self.kv_banks[h // 4]
                pb = (h % 2) * 64
                cb = ((h % 4) // 2) * 256
                key = (h // 4, pb)
                st_f = key not in kv_armed
                kv_armed.add(key)
                statp = vtm[:, (pjp * 4 + ptc) * D + h * DH:
                            (pjp * 4 + ptc) * D + (h + 1) * DH]
                mov = kp[:, h * 256:(h + 1) * 256]
                if hh < 7:
                    self.emit("t", lambda e, s=statp, m=mov, bank=bank,
                              pb=pb, cb=cb, a=st_f, b=last_tc:
                              e.matmul(bank[pb:pb + DS, cb:cb + 256],
                                       s, m, start=a, stop=b,
                                       skip_group_check=True))
                else:
                    vkv = self.inc("t", self.ts,
                                   lambda e, s=statp, m=mov, bank=bank,
                                   pb=pb, cb=cb, a=st_f, b=last_tc:
                                   e.matmul(bank[pb:pb + DS,
                                                 cb:cb + 256],
                                            s, m, start=a, stop=b,
                                            skip_group_check=True))
            kp_used[pu] = vkv

        pq1 = []
        for pair in range(4):
            j0, j1 = 2 * pair, 2 * pair + 1
            vst = self.emit_stats_pair(j0, j1, rowbank=self.pBC1)
            vln = [self.emit_normalize_tile(j0, 0, self.ss, vst),
                   self.emit_normalize_tile(j1, 1, self.ss, vst)]
            kfm_guard = self.ts.n if pair > 0 else 0
            for mk in range(8):
                par, vld = ld_k(mk)
                self.waitv("t", vld)
                for jp in (0, 1):
                    self.wait("t", self.vs, vln[jp])
                    ps = (self.pA, self.pB)[(mk * 2 + jp) % 2]
                    prs = [(self.stat_ap(self.wsl_b, par, 8 * 128, 128, kc, 0),
                            self.ln_bf[:, (jp * 8 + kc) * TC:
                                       (jp * 8 + kc + 1) * TC])
                           for kc in range(8)]
                    lm = self.mm_group(ps, prs)
                    self.wait("s", self.ts, lm)
                    self.wait("s", self.ts, kfm_guard)
                    vev = self.inc("s", self.ss, lambda e, p=ps, mk=mk, jp=jp,
                                   kfm=kfm: e.activation(
                                       out=kfm[:, (jp * 8 + mk) * TC:
                                               (jp * 8 + mk + 1) * TC],
                                       in_=p[:], func=AF.Identity,
                                       bias=self.bk_s[:, mk:mk + 1]))
                    self.pset(ps, "s", vev)
                used_k(par, self.ts.n)
            for pd in pq1:
                drain_kv(pd)
            pq1 = []
            vtm_guard = self.ts.n if pair > 0 else 0
            if self.attn_stop < 12:
                continue
            self.wait("s", self.gw, self.v_setup_gw)
            for half in range(2):
                parv, vldv = ld_v(half)
                self.waitv("t", vldv)
                for jp in (0, 1):
                    for tc in range(4):
                        ps = (self.pA, self.pB)[(tc + jp) % 2]
                        self.pwait(ps)
                        self.emit("t", lambda e, p=ps, half=half: e.matmul(
                            p[:], self.ones1_bf[:],
                            self.bv_bf[:, half * TC:(half + 1) * TC],
                            start=True, stop=False, skip_group_check=True))
                        lm = 0
                        for kc in range(8):
                            statp = self.ln_bf[:, (jp * 8 + kc) * TC + tc * 128:
                                               (jp * 8 + kc) * TC + (tc + 1) * 128]
                            mov = wvsl[:, parv * (8 * TC) + kc * TC:
                                       parv * (8 * TC) + (kc + 1) * TC]
                            if kc < 7:
                                self.emit("t", lambda e, p=ps, s=statp, m=mov:
                                          e.matmul(p[:], s, m, start=False,
                                                   stop=False,
                                                   skip_group_check=True))
                            else:
                                lm = self.inc("t", self.ts,
                                              lambda e, p=ps, s=statp, m=mov:
                                              e.matmul(p[:], s, m, start=False,
                                                       stop=True,
                                                       skip_group_check=True))
                        self.wait("s", self.ts, lm)
                        self.wait("s", self.ts, vtm_guard)
                        vev = self.inc("s", self.ss,
                                       lambda e, p=ps, jp=jp, tc=tc, half=half,
                                       vtm=vtm: e.activation(
                                           out=vtm[:, (jp * 4 + tc) * D +
                                                   half * TC:
                                                   (jp * 4 + tc) * D +
                                                   (half + 1) * TC],
                                           in_=p[:], func=AF.Copy))
                        self.pset(ps, "s", vev)
                used_v(parv, self.ts.n)
            if self.attn_stop < 13:
                continue
            for jp, j in enumerate((j0, j1)):
                for tc in range(4):
                    for u in range(2):
                        par = self.fs_par
                        self.fs_par = (self.fs_par + 1) % self.fs_npar
                        lm = 0
                        for uu in range(2):
                            hg = 2 * u + uu
                            ps = (self.pA, self.pB)[uu]
                            self.pwait(ps)
                            for hh in range(4):
                                h = hg * 4 + hh
                                g = h // 2
                                statp = kfm[:, (jp * 8 + g) * TC + tc * 128:
                                            (jp * 8 + g) * TC +
                                            (tc + 1) * 128]
                                mov = self.projz[:, (h % 2) * KM:
                                                 (h % 2 + 1) * KM]
                                if uu == 1 and hh == 3:
                                    lm = self.inc(
                                        "t", self.ts,
                                        lambda e, p=ps, s=statp, m=mov, hh=hh:
                                        e.matmul(
                                            p[:, hh * 128:(hh + 1) * 128],
                                            s, m, start=False, stop=True,
                                            skip_group_check=True))
                                else:
                                    self.emit(
                                        "t",
                                        lambda e, p=ps, s=statp, m=mov, hh=hh,
                                        a=(hh == 0), b=(hh == 3):
                                        e.matmul(
                                            p[:, hh * 128:(hh + 1) * 128],
                                            s, m, start=a, stop=b,
                                            skip_group_check=True))
                        if len(pq1) == 2:
                            drain_kv(pq1.pop(0))
                        vu = self.foursin_a(
                            [(self.pA[:], 0), (self.pB[:], TC)],
                            2 * TC, par, psum_wait=lm)
                        self.pset(self.pA, "s", vu)
                        self.pset(self.pB, "s", vu)
                        pq1.append((pair, jp, tc, u, par))
            self.lnbf_used[0] = self.ts.n
            self.lnbf_used[1] = self.ts.n

        if self.attn_stop < 14:
            sc1.close()
            self.sc_attn.close()
            return
        if self.attn_stop < 2:
            sc1.close()
            self.sc_attn.close()
            return
        for pd in pq1:
            drain_kv(pd)
        pq1 = []
        sc1.close()
        sc2 = ExitStack()
        kvp = sc2.enter_context(self.nc.sbuf_tensor("kvp", [128, 2048], F32))
        self.wait("s", self.ts, vkv)
        vev = 0
        for q in range(4):
            vev = self.inc("s", self.ss, lambda e, q=q, kvp=kvp: e.activation(
                out=kvp[:, q * TC:(q + 1) * TC],
                in_=self.kv_banks[q][:], func=AF.Copy))
            self.pset(self.kv_banks[q], "s", vev)
        self.wait("g", self.ss, vev)
        v1 = self.inc("g", self.ccb, lambda e, kvp=kvp: e.dma_start(
            out=self.kv_in.ap(), in_=kvp[:]))
        self.wait("g", self.ccb, v1)
        vcc = self.inc("g", self.cc, lambda e: e.collective_compute(
            "AllReduce", ALU.add, replica_groups=GROUPS,
            ins=[self.kv_in.ap().opt()], outs=[self.kv_out.ap().opt()]))
        self.wait("g", self.cc, vcc)
        v2 = self.inc("g", self.ccb, lambda e, kvp=kvp: e.dma_start(
            out=kvp[:], in_=self.kv_out.ap()))
        if self.dbg:
            self.wait("g", self.ccb, v2)
            vdb = self.inc("g", self.gw, lambda e, kvp=kvp: e.dma_start(
                out=self.dbg_kv, in_=kvp[:]))
            vdb = self.inc("g", self.gw, lambda e, kp=kp: e.dma_start(
                out=self.dbg_kp, in_=kp[:, NH * 256:2 * NH * 256]))
        self.wait("t", self.ccb, v2)
        self.wait("t", self.cst, self.v_bias)
        for h in range(NH):
            q = h // 4
            pb = (h % 2) * 64
            cb = ((h % 4) // 2) * 256
            for c in range(2):
                self.pwait(self.pA)
                vt = self.inc("t", self.ts, lambda e, q=q, pb=pb, cb=cb, c=c,
                              kvp=kvp: e.transpose(
                                  self.pA[:, 0:DH],
                                  kvp[pb:pb + DS, q * TC + cb + c * 128:
                                      q * TC + cb + (c + 1) * 128],
                                  self.eye_f[pb:pb + DS, :]))
                self.wait("s", self.ts, vt)
                vev = self.inc("s", self.ss, lambda e, h=h, c=c, kvS=kvS:
                               e.activation(
                                   out=kvS[:, (h * 2 + c) * DH:
                                           (h * 2 + c + 1) * DH],
                                   in_=self.pA[:, 0:DH], func=AF.Copy))
                self.pset(self.pA, "s", vev)
        self.v_kvS = self.ss.n
        sc2.close()
        if self.attn_stop < 3:
            self.sc_attn.close()
            return

        sc3 = ExitStack()
        qfm = sc3.enter_context(self.nc.sbuf_tensor("qfm", [128, 2 * 8 * TC], BF16))
        qph = sc3.enter_context(self.nc.sbuf_tensor("qph", [128, 2 * 4 * TC], BF16))
        afm = sc3.enter_context(self.nc.sbuf_tensor("afm", [128, 3 * 8 * TC], BF16))
        ld_q, used_q = self.slab_loader(self.wsl_b, 8 * 128, di["wq_t"], 8, 128)
        ld_o, used_o = self.slab_loader(self.wsl_a, 8 * 128, di["wo_t"], 8, 128)
        qfm_used = [0, 0]
        qph_used = [0, 0]
        self.wait("t", self.ss, self.v_kvS)

        vaf = 0
        pq = []
        afm_pend = []

        def drain_afm():
            nonlocal vaf
            php, eb, lm, tp = afm_pend.pop(0)
            self.wait("s", self.ts, lm)
            vaf = self.inc("s", self.ss, lambda e, hp=php, afm=afm, tp=tp,
                           eb=eb: e.activation(
                               out=afm[:, (tp * 8 + hp) * TC:
                                       (tp * 8 + hp + 1) * TC],
                               in_=eb[:], func=AF.Copy,
                               scale=1.0 / KM))
            self.pset(eb, "s", vaf)

        def drain_q(pd):
            ptp, php, ppar = pd
            qp = php % 2
            eb = (self.pBC3, self.pROW)[qp]
            q0 = qp * 4 * TC
            if len(afm_pend) == 2:
                drain_afm()
            vb = self.foursin_b(
                qph[:, q0:q0 + 2 * TC],
                qph[:, q0 + 2 * TC:q0 + 4 * TC],
                2 * TC, ppar, out_guard=(self.ts, qph_used[qp]))
            self.wait("t", self.vs, vb)
            self.pwait(eb)
            lm = 0
            for hh in range(2):
                h = php * 2 + hh
                r0 = hh * 64
                for c in range(2):
                    statp = kvS[:, (h * 2 + c) * DH:(h * 2 + c + 1) * DH]
                    mov = qph[:, q0 + c * 2 * TC + hh * TC:
                              q0 + c * 2 * TC + (hh + 1) * TC]
                    if hh == 1 and c == 1:
                        lm = self.inc("t", self.ts,
                                      lambda e, s=statp, m=mov, r0=r0,
                                      eb=eb:
                                      e.matmul(eb[r0:r0 + DS, :], s, m,
                                               start=False, stop=True,
                                               skip_group_check=True))
                    else:
                        self.emit("t", lambda e, s=statp, m=mov, r0=r0,
                                  eb=eb, a=(c == 0), b=(c == 1):
                                  e.matmul(eb[r0:r0 + DS, :], s, m,
                                           start=a, stop=b,
                                           skip_group_check=True))
            qph_used[qp] = lm
            afm_pend.append((php, eb, lm, ptp))

        def emit_wo_group(jj, mo):
            par, vld = ld_o(mo)
            self.waitv("t", vld)
            self.wait("t", self.ss, vaf)
            ps = (self.pA, self.pB)[mo % 2]
            prs = [(self.stat_ap(self.wsl_a, par, 8 * 128, 128, kc, 0),
                    afm[:, ((jj % 3) * 8 + kc) * TC:
                        ((jj % 3) * 8 + kc + 1) * TC])
                   for kc in range(8)]
            lm = self.mm_group(ps, prs)
            self.wait("v", self.ts, lm)
            vres = self.inc("v", self.vs, lambda e, p=ps, mo=mo, j=jj:
                            e.scalar_tensor_tensor(
                                self.st_ch(mo, j), p[:],
                                self.bo_s[:, mo:mo + 1],
                                self.st_ch(mo, j), ALU.add, ALU.add))
            self.pset(ps, "v", vres)
            used_o(par, self.ts.n)

        def emit_wo(jj):
            for mo in range(8):
                emit_wo_group(jj, mo)

        for j in range(NT):
            jp = j % 2
            vst = self._stats_one_tile(
                j, self.mean_pr[:, jp * TC:(jp + 1) * TC],
                self.rstd_pr[:, jp * TC:(jp + 1) * TC], rowbank=self.pBC1)
            vln = self.emit_normalize_tile(j, jp, self.ss, vst)
            vev_q = [0] * 8
            for mq in range(8):
                par, vld = ld_q(mq)
                self.waitv("t", vld)
                self.wait("t", self.vs, vln)
                ps = (self.pA, self.pB)[mq % 2]
                prs = [(self.stat_ap(self.wsl_b, par, 8 * 128, 128, kc, 0),
                        self.ln_bf[:, (jp * 8 + kc) * TC:
                                   (jp * 8 + kc + 1) * TC])
                       for kc in range(8)]
                lm = self.mm_group(ps, prs)
                self.wait("s", self.ts, lm)
                self.wait("s", self.ts, qfm_used[jp])
                vev = self.inc("s", self.ss, lambda e, p=ps, mq=mq, jp=jp,
                               qfm=qfm: e.activation(
                                   out=qfm[:, (jp * 8 + mq) * TC:
                                           (jp * 8 + mq + 1) * TC],
                                   in_=p[:], func=AF.Identity,
                                   bias=self.bq_s[:, mq:mq + 1]))
                self.pset(ps, "s", vev)
                vev_q[mq] = vev
                used_q(par, self.ts.n)
            self.lnbf_used[jp] = self.ts.n
            wo_q = [(j - 2, mo) for mo in range(8)] if j > 1 else []
            for hp in range(8):
                par = self.fs_par
                self.fs_par = (self.fs_par + 1) % self.fs_npar
                vb0, vb1 = ((self.pA, self.pB), (self.pC, self.pD))[hp % 2]
                lm = 0
                for hh in range(2):
                    h = hp * 2 + hh
                    ps = (vb0, vb1)[hh]
                    self.pwait(ps)
                    self.wait("t", self.ss, vev_q[hp])
                    fn = lambda e, p=ps, h=h, jp=jp, g=hp, qfm=qfm: e.matmul(
                        p[:], self.projz[:, (h % 2) * KM:(h % 2 + 1) * KM],
                        qfm[:, (jp * 8 + g) * TC:(jp * 8 + g + 1) * TC],
                        start=True, stop=True, skip_group_check=True)
                    if hh == 1:
                        lm = self.inc("t", self.ts, fn)
                    else:
                        self.emit("t", fn)
                if len(pq) == 2:
                    drain_q(pq.pop(0))
                vu = self.foursin_a([(vb0[:], 0), (vb1[:], TC)],
                                    2 * TC, par, psum_wait=lm)
                self.pset(vb0, "s", vu)
                self.pset(vb1, "s", vu)
                if wo_q:
                    emit_wo_group(*wo_q.pop(0))
                pq.append((j % 3, hp, par))
            qfm_used[jp] = self.ts.n
            if self.dbg and j == NT - 1:
                self.wait("g", self.ss, self.ss.n)
                self.inc("g", self.gw, lambda e, qph=qph: e.dma_start(
                    out=self.dbg_qph, in_=qph[:, TC:2 * TC]) if False else
                    e.dma_start(out=self.dbg_qph, in_=qph[:, 0:2 * TC]))
                self.inc("g", self.gw, lambda e, afm=afm: e.dma_start(
                    out=self.dbg_afm, in_=afm[:]))
        for pd in pq:
            drain_q(pd)
        while afm_pend:
            drain_afm()
        emit_wo(NT - 2)
        emit_wo(NT - 1)
        sc3.close()
        self.sc_attn.close()

    def foursin_a(self, srcs, ncol, par, psum_wait):
        wc, ws, ib = self.wcos[par], self.wsin[par], self.ibuf
        self.wait("s", self.vs, self.fs_vdone[par])
        self.wait("s", self.ts, psum_wait)
        vu = 0
        for p, off in srcs:
            vu = self.inc("s", self.ss, lambda e, p=p, off=off, wc=wc:
                          e.activation(out=wc[:, off:off + TC], in_=p,
                                       func=AF.Copy, scale=INV16PI))
        self.wait("v", self.ss, vu)
        self.emit("v", lambda e, wc=wc, ib=ib, n=ncol:
                  e.tensor_copy(ib[:, 0:n], wc[:, 0:n]))
        self.fs_v2[par] = self.inc(
            "v", self.vs, lambda e, wc=wc, ws=ws, ib=ib, n=ncol:
            e.tensor_tensor(ws[:, 0:n], wc[:, 0:n], ib[:, 0:n], ALU.subtract))
        return vu

    def foursin_b(self, cos_out, sin_out, ncol, par, out_guard=None):
        wc, ws = self.wcos[par], self.wsin[par]
        self.wait("s", self.vs, self.fs_v2[par])
        self.emit("s", lambda e, wc=wc, ws=ws, n=ncol: e.activation(
            out=wc[:, 0:n], in_=ws[:, 0:n], func=AF.Abs))
        self.emit("s", lambda e, ws=ws, n=ncol: e.activation(
            out=ws[:, 0:n], in_=ws[:, 0:n], func=AF.Sin, scale=math.pi))
        vs1 = self.inc("s", self.ss, lambda e, wc=wc, n=ncol: e.activation(
            out=wc[:, 0:n], in_=wc[:, 0:n], func=AF.Sin, scale=-math.pi,
            bias=self.c_hpi[:]))
        self.wait("v", self.ss, vs1)
        if out_guard is not None:
            self.waitv("v", out_guard)
        self.emit("v", lambda e, o=sin_out, wc=wc, ws=ws, n=ncol:
                  e.scalar_tensor_tensor(o, ws[:, 0:n], 2.0, wc[:, 0:n],
                                         ALU.mult, ALU.mult))
        self.emit("v", lambda e, wc=wc, ws=ws, n=ncol:
                  e.scalar_tensor_tensor(wc[:, 0:n], ws[:, 0:n], -2.0,
                                         ws[:, 0:n], ALU.mult, ALU.mult))
        v = self.inc("v", self.vs, lambda e, o=cos_out, wc=wc, n=ncol:
                     e.tensor_scalar(o, wc[:, 0:n], 1.0, None, ALU.add))
        self.fs_vdone[par] = v
        return v

    def emit_dump_stream(self):
        sc = ExitStack()
        dump = sc.enter_context(self.nc.sbuf_tensor("dump_f32", [128, T], F32))
        self.wait("s", self.vs, self.vs.n)
        self.wait("s", self.ts, self.ts.n)
        for mc in range(8):
            vc = self.inc("s", self.ss, lambda e, mc=mc, dump=dump:
                          e.activation(out=dump[:], in_=self.st_ch(mc),
                                       func=AF.Copy))
            self.wait("y", self.ss, vc)
            vst = self.inc("y", self.sy, lambda e, mc=mc, dump=dump:
                           e.dma_start(out=self.yt[mc * 128:(mc + 1) * 128, :],
                                       in_=dump[:]))
            self.wait("s", self.sy, vst)
        sc.close()

    def build(self):
        self.emit_setup()
        if self.upto >= 1:
            self.emit_mamba()
        if self.upto >= 2:
            self.emit_ffn(final=False)
        if self.upto >= 3:
            self.emit_attn()
        if self.upto >= 4:
            self.emit_ffn(final=True)
        else:
            self.emit_dump_stream()
        self.wait("g", self.sy, self.sy.n)
        self.wait("g", self.ost[0], self.ost[0].n)
        self.wait("g", self.ost[1], self.ost[1].n)

        block = self.es.enter_context(self.nc.Block())

        @block.sync
        def _(e):
            for fn in self.prog["y"]:
                fn(e)

        @block.gpsimd
        def _(e):
            for fn in self.prog["g"]:
                fn(e)

        @block.tensor
        def _(e):
            for fn in self.prog["t"]:
                fn(e)

        @block.scalar
        def _(e):
            for fn in self.prog["s"]:
                fn(e)

        @block.vector
        def _(e):
            for fn in self.prog["v"]:
                fn(e)

        self.es.close()
        return self.nc


def build_nc(upto=4, dbg=False, paranoid=0, rdbg=False, attn_stop=99):
    return Builder(upto=upto, dbg=dbg, paranoid=paranoid, rdbg=rdbg,
                   attn_stop=attn_stop).build()


def slab_major(W, kc, cols):
    K, M = W.shape
    assert K == kc * 128 and M % cols == 0
    t = W.reshape(kc, 128, M // cols, cols).transpose(1, 2, 0, 3)
    return np.ascontiguousarray(t.reshape(128, -1))


def make_in_maps(inputs):
    x = inputs["L0_attn_a_x"]
    C = np.ascontiguousarray

    def f32(a):
        return np.asarray(a, dtype=np.float32)
    base = {
        "w_in": slab_major(f32(inputs["L0_attn_a_W_in"]), 8, 256),
        "b_in_r": C(f32(inputs["L0_attn_a_b_in"]).reshape(32, 128).T),
        "w_dt": slab_major(f32(inputs["L0_attn_a_W_dt"]), 16, 128),
        "b_dt_r": C(f32(inputs["L0_attn_a_b_dt"]).reshape(8, 128).T),
        "w_xp": slab_major(f32(inputs["L0_attn_a_W_xp"]), 16, 64),
        "b_xp_r": C(f32(inputs["L0_attn_a_b_xp"]).reshape(DS, 1)),
        "w_out": slab_major(f32(inputs["L0_attn_a_W_out"]), 16, 128),
        "b_out_r": C(f32(inputs["L0_attn_a_b_out"]).reshape(8, 128).T),
        "a_log_r": C(f32(inputs["L0_attn_a_A_log"]).reshape(DS, 1)),
        "d_r": C(f32(inputs["L0_attn_a_D"]).reshape(8, 128).T),
        "proj": C(f32(inputs["L0_attn_b_proj_mat"])),
        "wq_t": slab_major(f32(inputs["L0_attn_b_Wq"]).T, 8, 128),
        "wk_t": slab_major(f32(inputs["L0_attn_b_Wk"]).T, 8, 128),
        "wv_t": slab_major(f32(inputs["L0_attn_b_Wv"]).T, 8, 512),
        "wo_t": slab_major(f32(inputs["L0_attn_b_Wo"]).T, 8, 128),
        "bq_r": C(f32(inputs["L0_attn_b_bq"]).reshape(8, 128).T),
        "bk_r": C(f32(inputs["L0_attn_b_bk"]).reshape(8, 128).T),
        "bo_r": C(f32(inputs["L0_attn_b_bo"]).reshape(8, 128).T),
        "bv_row": C(f32(inputs["L0_attn_b_bv"]).reshape(1, D)),
        "wg_t": slab_major(np.asarray(inputs["L0_ffn_w_gate"], np.int32).T, 8, 128),
        "wu_t": slab_major(np.asarray(inputs["L0_ffn_w_up"], np.int32).T, 8, 128),
        "wd_t": slab_major(np.asarray(inputs["L0_ffn_w_down"], np.int32).T, 24, 128),
        "eye": np.concatenate([np.eye(DS, dtype=np.float32),
                               np.eye(DS, dtype=np.float32)], axis=0),
    }
    in_maps = []
    for c in range(8):
        b, h = c // 2, c % 2
        m = dict(base)
        m["xt"] = C(f32(x[b, h * T:(h + 1) * T, :]).T)
        m["isfirst"] = np.full((DS, 1), 1.0 if h == 0 else 0.0, np.float32)
        m["notfirst"] = np.full((DS, 1), 0.0 if h == 0 else 1.0, np.float32)
        in_maps.append(m)
    return in_maps


def kernel(**inputs):
    nc = build_nc(upto=4)
    in_maps = make_in_maps(inputs)
    res = run_bass_kernel_spmd(nc, in_maps, core_ids=list(range(8)))
    out = np.empty((B, S, D), dtype=np.float32)
    for c in range(8):
        b, h = c // 2, c % 2
        out[b, h * T:(h + 1) * T, :] = np.asarray(res.results[c]["yt"]).T
    return out


# version 12 (final)
# speedup vs baseline: 1.0830x; 1.0830x over previous
import math
from contextlib import ExitStack

import numpy as np
import concourse.bass as bass
import concourse.mybir as mybir
import concourse.bass_utils as _bass_utils
from concourse.bass_utils import run_bass_kernel_spmd

if not getattr(_bass_utils, "_ldw_opt_patched", False):
    _orig_run_command = _bass_utils.run_command

    def _run_command_ldw(cmd, *a, **kw):
        cmd = [c.replace("--enable-ldw-opt=false", "--enable-ldw-opt=true")
               if isinstance(c, str) else c for c in cmd]
        return _orig_run_command(cmd, *a, **kw)

    _bass_utils.run_command = _run_command_ldw
    _bass_utils._ldw_opt_patched = True

F32 = mybir.dt.float32
BF16 = mybir.dt.bfloat16
I32 = mybir.dt.int32
AF = mybir.ActivationFunctionType
ALU = mybir.AluOpType

B, S, D = 4, 8192, 1024
T = 4096
TC = 512
NT = T // TC
DI = 2048
DS = 64
HID = 3072
NH, DH, KM = 16, 64, 128
QS = 1.0 / 127.0
EPS = 1e-5
INV16PI = 1.0 / (16.0 * math.pi)
TWO_PI = 2.0 * math.pi
GROUPS = [[0, 1], [2, 3], [4, 5], [6, 7]]


class Sem:
    def __init__(self, handle, step):
        self.h = handle
        self.step = step
        self.n = 0

    def bump(self):
        self.n += self.step
        return self.n


class Builder:
    def __init__(self, upto=4, dbg=False, paranoid=0, rdbg=False,
                 attn_stop=99):
        self.attn_stop = attn_stop
        self.upto = upto
        self.dbg = dbg
        self.paranoid = paranoid
        self.rdbg = rdbg
        nc = bass.Bass("TRN2", target_bir_lowering=False,
                       detect_race_conditions=rdbg)
        self.nc = nc
        self.prog = {"t": [], "s": [], "v": [], "g": [], "y": []}
        self.es = ExitStack()

        di = {}

        def inp(name, shape, dt=F32):
            di[name] = nc.dram_tensor(name, list(shape), dt,
                                      kind="ExternalInput").ap()
        inp("xt", (D, T))
        inp("w_in", (128, 8 * 2 * DI)); inp("b_in_r", (128, 32))
        inp("w_dt", (128, 16 * D)); inp("b_dt_r", (128, 8))
        inp("w_xp", (128, 16 * DS)); inp("b_xp_r", (DS, 1))
        inp("w_out", (128, 16 * D)); inp("b_out_r", (128, 8))
        inp("a_log_r", (DS, 1)); inp("d_r", (128, 8))
        inp("proj", (DH, KM))
        for wn in ("wq_t", "wk_t", "wv_t", "wo_t"):
            inp(wn, (128, 8 * D))
        inp("bq_r", (128, 8)); inp("bk_r", (128, 8)); inp("bo_r", (128, 8))
        inp("bv_row", (1, D))
        inp("wg_t", (128, 8 * HID), I32); inp("wu_t", (128, 8 * HID), I32)
        inp("wd_t", (128, 24 * D), I32)
        inp("eye", (128, DS))
        inp("isfirst", (DS, 1)); inp("notfirst", (DS, 1))
        self.di = di
        self.yt = nc.dram_tensor("yt", [D, T], F32, kind="ExternalOutput").ap()

        self.agx_in = nc.dram_tensor("agx_in", [DS, 1], F32)
        self.agx_out = nc.dram_tensor("agx_out", [2 * DS, 1], F32)
        self.agy_in = nc.dram_tensor("agy_in", [DS, 1], F32)
        self.agy_out = nc.dram_tensor("agy_out", [2 * DS, 1], F32)
        self.agz_in = nc.dram_tensor("agz_in", [DS, 1], F32)
        self.agz_out = nc.dram_tensor("agz_out", [2 * DS, 1], F32)
        self.kv_in = nc.dram_tensor("kv_in", [128, 2048], F32)
        self.kv_out = nc.dram_tensor("kv_out", [128, 2048], F32)
        self.pdram = nc.dram_tensor("pdram", [128, 8 * 8 * TC], BF16)
        if getattr(self, "dbg", False):
            self.dbg_kv = nc.dram_tensor("dbg_kv", [128, 2048], F32,
                                         kind="ExternalOutput").ap()
            self.dbg_kp = nc.dram_tensor("dbg_kp", [128, NH * 256], F32,
                                         kind="ExternalOutput").ap()
            self.dbg_qph = nc.dram_tensor("dbg_qph", [128, 1024], F32,
                                          kind="ExternalOutput").ap()
            self.dbg_afm = nc.dram_tensor("dbg_afm", [128, 8 * TC], F32,
                                          kind="ExternalOutput").ap()
        self.sdram = nc.dram_tensor("sdram", [2, T], F32)

        ec = self.es.enter_context
        self._ec = ec
        self._eng_sems = {k: [] for k in ("t", "s", "v", "g", "y")}
        self._nsem = [0]

        def mksem(eng, step):
            s = Sem(ec(nc.semaphore("sm%d" % self._nsem[0])), step)
            self._nsem[0] += 1
            self._eng_sems[eng].append(s)
            return s
        self.mksem = mksem
        self.sy = mksem("y", 16)
        self.gw = mksem("g", 16)
        self.cc = mksem("g", 1)
        self.ts = mksem("t", 1)
        self.ss = mksem("s", 1)
        self.vs = mksem("v", 1)
        self.gs = mksem("g", 1)
        self.xls = mksem("g", 16)
        self.cst = mksem("y", 16)
        self.ccb = mksem("g", 16)
        self.srw = mksem("y", 16)
        self.sld = [mksem("y", 16), mksem("y", 16)]
        self.pst = [mksem("y", 16), mksem("y", 16)]
        self.pld = [mksem("y", 16), mksem("y", 16)]
        self.ost = [mksem("y", 16), mksem("y", 16)]

        def sb(name, p, f, dt):
            return ec(nc.sbuf_tensor(name, [p, f], dt))
        self.stream = sb("stream", 128, 8 * T, BF16)
        self.ln_bf = sb("ln_bf", 128, 2 * 8 * TC, BF16)
        self.sq_scr = sb("sq_scr", 128, 2 * TC, BF16)
        self.sq_st = sb("sq_st", 128, 2 * TC, BF16)
        self.cscr = sb("cscr", 128, TC, F32)
        self.mean_pr = sb("mean_pr", 1, 2 * TC, F32)
        self.rstd_pr = sb("rstd_pr", 1, 2 * TC, F32)
        self.rowA = sb("rowA", 1, TC, F32)
        self.rowB = sb("rowB", 1, TC, F32)
        self.b_in_s = sb("b_in_s", 128, 32, F32)
        self.b_dt_s = sb("b_dt_s", 128, 8, F32)
        self.nb_dt_s = sb("nb_dt_s", 128, 8, F32)
        self.b_xp_s = sb("b_xp_s", DS, 1, F32)
        self.b_out_s = sb("b_out_s", 128, 8, F32)
        self.a_log_s = sb("a_log_s", DS, 1, F32)
        self.aexp_s = sb("aexp_s", DS, 1, F32)
        self.d_s = sb("d_s", 128, 8, F32)
        self.bq_s = sb("bq_s", 128, 8, F32)
        self.bk_s = sb("bk_s", 128, 8, F32)
        self.bo_s = sb("bo_s", 128, 8, F32)
        self.bv_bf = sb("bv_bf", 1, D, BF16)
        self.projz = sb("projz", 128, 2 * KM, BF16)
        self.eye_f = sb("eye_f", 128, DS, F32)
        self.identf = sb("identf", 1, 1, F32)
        self.onesb = sb("onesb", 128, 1, BF16)
        self.ones64f = sb("ones64f", DS, 1, F32)
        self.onesbcf = sb("onesbcf", 1, 128, F32)
        self.ones64bc = sb("ones64bc", 1, DS, F32)
        self.ones1_bf = sb("ones1_bf", 1, 128, BF16)
        self.zeros64 = sb("zeros64", DS, 1, F32)
        self.ones64 = sb("ones64", DS, 1, F32)
        self.c_eps = sb("c_eps", 1, 1, F32)
        self.c_075 = sb("c_075", 128, 1, F32)
        self.c_hpi = sb("c_hpi", 128, 1, F32)
        self.c_negpi = sb("c_negpi", 128, 1, F32)
        self.isf_s = sb("isf_s", DS, 1, F32)
        self.nf_s = sb("nf_s", DS, 1, F32)
        self.recvX = sb("recvX", DS, 1, F32)
        self.recvY = sb("recvY", DS, 1, F32)
        self.recvZ = sb("recvZ", DS, 1, F32)
        self.cvec = sb("cvec", DS, 8, F32)
        self.wsl_a = sb("wsl_a", 128, 2 * 8 * 256, BF16)
        self.wsl_b = sb("wsl_b", 128, 2 * 16 * 128, BF16)
        self.nrm_scr = self.cscr

        self.pA = ec(nc.psum_tensor("pA", [128, TC], F32))
        self.pB = ec(nc.psum_tensor("pB", [128, TC], F32))
        self.pC = ec(nc.psum_tensor("pC", [128, TC], F32))
        self.pD = ec(nc.psum_tensor("pD", [128, TC], F32))
        self.pROW = ec(nc.psum_tensor("pROW", [128, TC], F32))
        self.pBC1 = ec(nc.psum_tensor("pBC1", [128, TC], F32))
        self.pBC2 = ec(nc.psum_tensor("pBC2", [128, TC], F32))
        self.pBC3 = ec(nc.psum_tensor("pBC3", [128, TC], F32))
        self.kv_banks = [self.pC, self.pD, self.pROW, self.pBC3]

        if self.rdbg:
            self.esem = {k: Sem(ec(nc.semaphore("esem_" + k)), 1)
                         for k in ("t", "s", "v", "g", "y")}
        self.pguard = {}
        self._last_nrm = 0
        self.lnbf_used = [0, 0]
        self._pnames = {id(p): n for p, n in
                        [(self.pA, "pA"), (self.pB, "pB"), (self.pC, "pC"),
                         (self.pD, "pD"), (self.pROW, "pROW"),
                         (self.pBC1, "pBC1"), (self.pBC2, "pBC2"),
                         (self.pBC3, "pBC3")]}

    def barrier(self, level=1, tag=None):
        import os
        tags = set(os.environ.get("PARANOID_TAGS", "").split(","))
        if self.paranoid < level and (tag is None or tag not in tags):
            return
        sems = [self.ts, self.ss, self.vs, self.gw, self.sy, self.cc, self.gs]
        for eng in ("t", "s", "v", "g", "y"):
            for sm in sems:
                self.wait(eng, sm, sm.n)

    def wait(self, eng, sem, val):
        if val and val > 0:
            self.prog[eng].append(lambda e, h=sem.h, v=val: e.wait_ge(h, v))

    def _rdbg_prewait(self, eng):
        es = self.esem[eng]
        if es.n:
            self.prog[eng].append(lambda e, h=es.h, v=es.n: e.wait_ge(h, v))
        for sm in self._eng_sems[eng]:
            if sm.n:
                self.prog[eng].append(
                    lambda e, h=sm.h, v=sm.n: e.wait_ge(h, v))

    def emit(self, eng, fn, has_inc=False):
        if self.rdbg:
            self._rdbg_prewait(eng)
            if has_inc:
                self.prog[eng].append(fn)
            else:
                es = self.esem[eng]
                self.prog[eng].append(
                    lambda e, f=fn, h=es.h: f(e).then_inc(h, 1))
                es.n += 1
        else:
            self.prog[eng].append(fn)

    def inc(self, eng, sem, fn):
        if self.rdbg and sem.step == 16 or (self.rdbg and sem is self.cc):
            self.wait(eng, sem, sem.n)
        self.emit(eng, lambda e, f=fn, h=sem.h, st=sem.step:
                  f(e).then_inc(h, st), has_inc=True)
        return sem.bump()

    def st_ch(self, mc, j=None, cols=TC):
        if j is None:
            return self.stream[:, mc * T:(mc + 1) * T]
        c0 = mc * T + j * TC
        return self.stream[:, c0:c0 + cols]

    def waitv(self, eng, sv):
        sem, val = sv
        self.wait(eng, sem, val)

    def pwait(self, ps):
        g = self.pguard.get(id(ps))
        if g:
            eng, val = g
            sem = {"s": self.ss, "v": self.vs}[eng]
            self.wait("t", sem, val)

    def pset(self, ps, eng, val):
        self.pguard[id(ps)] = (eng, val)

    def mm_group(self, ps, pairs, arm=True, close=True, guard=True):
        if guard:
            self.pwait(ps)
        n = len(pairs)
        last = 0
        for i, (stat, mov) in enumerate(pairs):
            st_f = arm and (i == 0)
            sp_f = close and (i == n - 1)
            if i < n - 1:
                self.emit("t", lambda e, p=ps, s=stat, m=mov, a=st_f, b=sp_f:
                          e.matmul(p[:], s, m, start=a, stop=b,
                                   skip_group_check=True))
            else:
                last = self.inc("t", self.ts,
                                lambda e, p=ps, s=stat, m=mov, a=st_f, b=sp_f:
                                e.matmul(p[:], s, m, start=a, stop=b,
                                         skip_group_check=True))
        return last

    def emit_setup(self):
        di = self.di
        for name, dst in [("b_in_r", self.b_in_s), ("b_dt_r", self.b_dt_s),
                          ("b_xp_r", self.b_xp_s), ("b_out_r", self.b_out_s),
                          ("a_log_r", self.a_log_s), ("d_r", self.d_s),
                          ("bq_r", self.bq_s), ("bk_r", self.bk_s),
                          ("bo_r", self.bo_s), ("eye", self.eye_f),
                          ("isfirst", self.isf_s), ("notfirst", self.nf_s)]:
            self.inc("y", self.cst,
                     lambda e, a=di[name], d=dst: e.dma_start(out=d[:], in_=a))
        self.v_bias = self.cst.n
        self.bias_sem = self.cst
        self.inc("g", self.gs, lambda e: e.memset(self.projz[:], 0.0))
        self.inc("g", self.gw, lambda e: e.dma_start(out=self.projz[0:DH, 0:KM],
                                                     in_=di["proj"]))
        self.inc("g", self.gw, lambda e: e.dma_start(
            out=self.projz[DH:128, KM:2 * KM], in_=di["proj"]))
        self.inc("g", self.gw, lambda e: e.dma_start(out=self.bv_bf[:],
                                                     in_=di["bv_row"]))
        self.v_setup_gw = self.gw.n
        for mc in range(8):
            self.inc("g", self.xls, lambda e, mc=mc: e.dma_start(
                out=self.st_ch(mc), in_=di["xt"][mc * 128:(mc + 1) * 128, :]))
        self.v_xload = self.xls.n
        for dst, val in [(self.onesb, 1.0), (self.ones64f, 1.0),
                         (self.onesbcf, 1.0), (self.ones64bc, 1.0 / 1024.0),
                         (self.ones1_bf, 1.0), (self.zeros64, 0.0),
                         (self.ones64, 1.0),
                         (self.identf, 1.0), (self.c_eps, EPS),
                         (self.c_075, 0.75), (self.c_negpi, -math.pi),
                         (self.c_hpi, math.pi / 2)]:
            self.inc("g", self.gs, lambda e, d=dst, c=val: e.memset(d[:], c))
        self.v_memset = self.gs.n
        self.wait("s", self.cst, self.v_bias)
        self.inc("s", self.ss, lambda e: e.mul(out=self.nb_dt_s[:],
                                               in_=self.b_dt_s[:], mul=-1.0))
        self.inc("s", self.ss, lambda e: e.activation(
            out=self.aexp_s[:], in_=self.a_log_s[:], func=AF.Exp))
        self.v_consts = self.ss.n
        self._first_stats = True

    def _stats_one_tile(self, j, dst_mean, dst_rstd, rowbank=None):
        rb = rowbank if rowbank is not None else self.pROW
        if self._first_stats:
            self.wait("t", self.gs, self.v_memset)
            self.wait("s", self.xls, self.v_xload)
            self._first_stats = False
        self.pwait(rb)
        self.wait("s", self.vs, self.vs.n)
        sq_done = []
        for mc in range(8):
            par = mc % 2
            if mc >= 2:
                self.wait("s", self.ts, sq_done[mc - 2])
            vsq = self.inc("s", self.ss, lambda e, mc=mc, j=j, par=par:
                           e.activation(
                               out=self.sq_st[:, par * TC:(par + 1) * TC],
                               in_=self.st_ch(mc, j), func=AF.Square))
            self.wait("t", self.ss, vsq)
            self.emit("t", lambda e, mc=mc, j=j, rb=rb: e.matmul(
                rb[0:1, :], self.onesb[:], self.st_ch(mc, j),
                start=(mc == 0), stop=(mc == 7), skip_group_check=True))
            vmm = self.inc("t", self.ts, lambda e, par=par, mc=mc, rb=rb:
                           e.matmul(
                               rb[32:33, :], self.onesb[:],
                               self.sq_st[:, par * TC:(par + 1) * TC],
                               start=(mc == 0), stop=(mc == 7),
                               skip_group_check=True))
            sq_done.append(vmm)
        self.wait("s", self.ts, sq_done[-1])
        self.inc("s", self.ss, lambda e, d=dst_mean, rb=rb: e.activation(
            out=d, in_=rb[0:1, :], func=AF.Copy, scale=1.0 / 1024.0))
        vr = self.inc("s", self.ss, lambda e, rb=rb: e.activation(
            out=self.rowA[:], in_=rb[32:33, :], func=AF.Copy,
            scale=1.0 / 1024.0))
        self.pset(rb, "s", vr)
        vq2 = self.inc("s", self.ss, lambda e, d=dst_mean: e.activation(
            out=self.rowB[:], in_=d, func=AF.Square))
        self.wait("v", self.ss, vq2)
        vv = self.inc("v", self.vs, lambda e: e.tensor_tensor(
            self.rowA[:], self.rowA[:], self.rowB[:], ALU.subtract))
        self.wait("s", self.vs, vv)
        self.inc("s", self.ss, lambda e: e.activation(
            out=self.rowB[:], in_=self.rowA[:], func=AF.Ln, bias=self.c_eps[:]))
        return self.inc("s", self.ss, lambda e, d=dst_rstd: e.activation(
            out=d, in_=self.rowB[:], func=AF.Exp, scale=-0.5))

    def emit_stats_pair(self, j0, j1=None, rowbank=None):
        v = self._stats_one_tile(j0, self.mean_pr[:, 0:TC],
                                 self.rstd_pr[:, 0:TC], rowbank)
        if j1 is not None:
            v = self._stats_one_tile(j1, self.mean_pr[:, TC:2 * TC],
                                     self.rstd_pr[:, TC:2 * TC], rowbank)
        return v

    def emit_normalize_tile(self, j, jp, rows_sem, rows_val):
        self.wait("t", rows_sem, rows_val)
        self.wait("t", self.vs, self._last_nrm)
        self.wait("t", self.ts, self.lnbf_used[jp])
        self.pwait(self.pBC1)
        self.pwait(self.pBC2)
        self.emit("t", lambda e, jp=jp: e.matmul(
            self.pBC1[:], self.onesbcf[:],
            self.mean_pr[:, jp * TC:(jp + 1) * TC],
            start=True, stop=True, skip_group_check=True))
        vbc = self.inc("t", self.ts, lambda e, jp=jp: e.matmul(
            self.pBC2[:], self.onesbcf[:],
            self.rstd_pr[:, jp * TC:(jp + 1) * TC],
            start=True, stop=True, skip_group_check=True))
        self.wait("v", self.ts, vbc)
        x3 = self.stream[:].rearrange("p (m t) -> p m t", m=8)[
            :, :, j * TC:(j + 1) * TC]
        l3 = self.ln_bf[:, jp * 8 * TC:(jp + 1) * 8 * TC].rearrange(
            "p (m t) -> p m t", m=8)
        self.emit("v", lambda e, x3=x3, l3=l3: e.tensor_tensor(
            l3, x3, self.pBC1[:].unsqueeze(1).broadcast_to([128, 8, TC]),
            ALU.subtract))
        last = self.inc("v", self.vs, lambda e, l3=l3: e.tensor_tensor(
            l3, l3, self.pBC2[:].unsqueeze(1).broadcast_to([128, 8, TC]),
            ALU.mult))
        self._last_nrm = last
        self.pset(self.pBC1, "v", last)
        self.pset(self.pBC2, "v", last)
        return last

    def slab_loader(self, buf, slab_elems, dram_ap, kc, slab_cols,
                    double=True, nbuf=2):
        nb = nbuf if double else 1
        sems = [self.mksem("g", 16) for _ in range(nb)]
        state = {"used": [self.ts.n] * nb, "seq": 0}
        w = kc * slab_cols

        def load(slab_no):
            par = (state["seq"] % nb) if double else 0
            state["seq"] += 1
            if state["used"][par]:
                self.wait("g", self.ts, state["used"][par])
            c0 = slab_no * w
            out_ap = buf[:, par * slab_elems:par * slab_elems + w]
            v = self.inc("g", sems[par],
                         lambda e, o=out_ap, a=dram_ap[:, c0:c0 + w]:
                         e.dma_start(out=o, in_=a))
            return par, (sems[par], v)

        def used(par, ts_val):
            state["used"][par] = ts_val
        return load, used

    def stat_ap(self, buf, par, slab_elems, kc_cols, kc, off, width=128):
        c0 = par * slab_elems + kc * kc_cols + off
        return buf[:, c0:c0 + width]

    def emit_mamba(self):
        di = self.di
        sc = ExitStack()
        xi = sc.enter_context(self.nc.sbuf_tensor("xi", [128, 2 * 16 * TC], BF16))
        sgm = sc.enter_context(self.nc.sbuf_tensor("sgm", [128, 8 * TC], BF16))
        sloc = sc.enter_context(self.nc.sbuf_tensor("sloc", [DS, NT * TC], F32))
        apfx = sc.enter_context(self.nc.sbuf_tensor("apfx", [DS, NT * TC], BF16))
        a_t = sc.enter_context(self.nc.sbuf_tensor("a_t", [DS, TC], F32))
        bx_t = sc.enter_context(self.nc.sbuf_tensor("bx_t", [DS, TC], F32))
        scorr = sc.enter_context(self.nc.sbuf_tensor("scorr", [DS, TC], F32))
        pstage = sc.enter_context(self.nc.sbuf_tensor("pstage", [128, 2 * TC], BF16))
        self.wsl_c = sc.enter_context(
            self.nc.sbuf_tensor("wsl_cm", [128, 2 * 16 * 128], BF16))
        self.wsl_d = sc.enter_context(
            self.nc.sbuf_tensor("wsl_d", [128, 16 * 64], BF16))
        self.zscan = sc.enter_context(
            self.nc.sbuf_tensor("zscan", [DS, TC], F32))
        vzs = self.inc("g", self.gs, lambda e: e.memset(self.zscan[:], 0.0))

        ld_in, used_in = self.slab_loader(self.wsl_a, 8 * 256, di["w_in"], 8, 256)
        ld_dt, used_dt = self.slab_loader(self.wsl_b, 16 * 128, di["w_dt"], 16, 128)
        ld_xp, _ = self.slab_loader(self.wsl_d, 16 * 64, di["w_xp"], 16, 64,
                                    double=False)
        ld_out, used_out = self.slab_loader(self.wsl_c, 16 * 128, di["w_out"], 16, 128)
        scan_done = [0] * NT
        pst_free = [None, None]
        xp_loaded = [None]

        vst_pre = None
        for pair in range(4):
            j0, j1 = 2 * pair, 2 * pair + 1
            vst = vst_pre if vst_pre is not None \
                else self.emit_stats_pair(j0, j1)
            vln = [self.emit_normalize_tile(j0, 0, self.ss, vst),
                   self.emit_normalize_tile(j1, 1, self.ss, vst)]
            for jp, j in ((0, j0), (1, j1)):
                for mc in range(8):
                    vln[jp] = self.inc("v", self.vs, lambda e, mc=mc, j=j, jp=jp:
                                       e.scalar_tensor_tensor(
                                           self.st_ch(mc, j),
                                           self.ln_bf[:, (jp * 8 + mc) * TC:
                                                      (jp * 8 + mc + 1) * TC],
                                           self.d_s[:, mc:mc + 1],
                                           self.st_ch(mc, j),
                                           ALU.mult, ALU.add))

            self.barrier(1)
            for sl in range(8):
                par, vld = ld_in(sl)
                self.waitv("t", vld)
                for half in range(2):
                    mc = sl * 2 + half
                    for jp in (0, 1):
                        self.wait("t", self.vs, vln[jp])
                        ps = (self.pA, self.pB)[(mc * 2 + jp) % 2]
                        prs = [(self.stat_ap(self.wsl_a, par, 8 * 256, 256,
                                             kc, half * 128),
                                self.ln_bf[:, (jp * 8 + kc) * TC:
                                           (jp * 8 + kc + 1) * TC])
                               for kc in range(8)]
                        lm = self.mm_group(ps, prs)
                        self.wait("s", self.ts, lm)
                        vev = self.inc("s", self.ss,
                                       lambda e, p=ps, mc=mc, jp=jp:
                                       e.activation(
                                           out=xi[:, (jp * 16 + mc) * TC:
                                                  (jp * 16 + mc + 1) * TC],
                                           in_=p[:], func=AF.Silu,
                                           bias=self.b_in_s[:, mc:mc + 1]))
                        self.pset(ps, "s", vev)
                used_in(par, self.ts.n)
                self.barrier(2, "A")

            self.barrier(1)
            for jp, j in enumerate((j0, j1)):
                for md in range(8):
                    par, vld = ld_dt(md)
                    self.waitv("t", vld)
                    ps = (self.pA, self.pB)[md % 2]
                    prs = [(self.stat_ap(self.wsl_b, par, 16 * 128, 128, kc, 0),
                            xi[:, (jp * 16 + kc) * TC:(jp * 16 + kc + 1) * TC])
                           for kc in range(16)]
                    lm = self.mm_group(ps, prs)
                    self.wait("s", self.ts, lm)
                    vsg = self.inc("s", self.ss, lambda e, p=ps, md=md:
                                   e.activation(
                                       out=sgm[:, md * TC:(md + 1) * TC],
                                       in_=p[:], func=AF.Sigmoid, scale=-1.0,
                                       bias=self.nb_dt_s[:, md:md + 1]))
                    self.pset(ps, "s", vsg)
                    used_dt(par, self.ts.n)
                    self.barrier(2, "B")
                self.pwait(self.pROW)
                vmd = 0
                for md in range(8):
                    vl = self.inc("s", self.ss, lambda e, md=md: e.activation(
                        out=sgm[:, md * TC:(md + 1) * TC],
                        in_=sgm[:, md * TC:(md + 1) * TC], func=AF.Ln))
                    self.wait("t", self.ss, vl)
                    if md < 7:
                        self.emit("t", lambda e, md=md: e.matmul(
                            self.pROW[64:65, :], self.onesb[:],
                            sgm[:, md * TC:(md + 1) * TC],
                            start=(md == 0), stop=False,
                            skip_group_check=True))
                    else:
                        vmd = self.inc("t", self.ts, lambda e: e.matmul(
                            self.pROW[64:65, :], self.onesb[:],
                            sgm[:, 7 * TC:8 * TC],
                            start=False, stop=True, skip_group_check=True))
                self.wait("s", self.ts, vmd)
                vrow = self.inc("s", self.ss, lambda e: e.activation(
                    out=self.rowA[:], in_=self.pROW[64:65, :], func=AF.Copy))
                self.pset(self.pROW, "s", vrow)
                self.pwait(self.pBC3)
                self.wait("t", self.ss, vrow)
                vbc = self.inc("t", self.ts, lambda e: e.matmul(
                    self.pBC3[0:DS, :], self.ones64bc[:], self.rowA[:],
                    start=True, stop=True, skip_group_check=True))
                self.wait("s", self.ts, vbc)
                self.wait("s", self.ss, self.v_consts)
                va = self.inc("s", self.ss, lambda e, a_t=a_t: e.activation(
                    out=a_t[:], in_=self.pBC3[0:DS, :], func=AF.Exp,
                    scale=self.aexp_s[:]))
                self.pset(self.pBC3, "s", va)
                if xp_loaded[0] is None:
                    xp_loaded[0] = ld_xp(0)
                    self.waitv("t", xp_loaded[0][1])
                self.pwait(self.pBC3)
                lm = 0
                for kc in range(16):
                    statp = self.wsl_d[:, kc * 64:(kc + 1) * 64]
                    mov = xi[:, (jp * 16 + kc) * TC:(jp * 16 + kc + 1) * TC]
                    if kc < 15:
                        self.emit("t", lambda e, s=statp, m=mov, a=(kc == 0):
                                  e.matmul(self.pBC3[0:DS, :], s, m, start=a,
                                           stop=False, skip_group_check=True))
                    else:
                        lm = self.inc("t", self.ts, lambda e, s=statp, m=mov:
                                      e.matmul(self.pBC3[0:DS, :], s, m,
                                               start=False, stop=True,
                                               skip_group_check=True))
                self.wait("s", self.ts, lm)
                vbx = self.inc("s", self.ss, lambda e, bx_t=bx_t: e.activation(
                    out=bx_t[:], in_=self.pBC3[0:DS, :], func=AF.Identity,
                    bias=self.b_xp_s[:]))
                self.pset(self.pBC3, "s", vbx)
                self.wait("v", self.ss, max(va, vbx))
                self.wait("v", self.gs, vzs)
                if j % 4 == 0:
                    init_s, init_a = self.zeros64[:, 0:1], self.ones64[:, 0:1]
                else:
                    init_s = sloc[:, j * TC - 1:j * TC]
                    init_a = apfx[:, j * TC - 1:j * TC]
                self.emit("v", lambda e, j=j, i=init_s, a_t=a_t, bx_t=bx_t,
                          sloc=sloc: e.tensor_tensor_scan(
                              sloc[:, j * TC:(j + 1) * TC], a_t[:], bx_t[:],
                              i, ALU.mult, ALU.add))
                vsc = self.inc("v", self.vs, lambda e, j=j, i=init_a, a_t=a_t,
                               apfx=apfx: e.tensor_tensor_scan(
                                   apfx[:, j * TC:(j + 1) * TC], a_t[:],
                                   self.zscan[:], i, ALU.mult, ALU.add))
                scan_done[j] = vsc
                self.wait("s", self.vs, vsc)
                self.barrier(2, "C")

            self.barrier(1)
            for sl in range(8, 16):
                par, vld = ld_in(sl)
                self.waitv("t", vld)
                for half in range(2):
                    mc = (sl - 8) * 2 + half
                    for jp in (0, 1):
                        ps = (self.pA, self.pB)[(mc * 2 + jp) % 2]
                        prs = [(self.stat_ap(self.wsl_a, par, 8 * 256, 256,
                                             kc, half * 128),
                                self.ln_bf[:, (jp * 8 + kc) * TC:
                                           (jp * 8 + kc + 1) * TC])
                               for kc in range(8)]
                        lm = self.mm_group(ps, prs)
                        self.wait("s", self.ts, lm)
                        self.wait("s", self.vs, self.vs.n)
                        vev = self.inc("s", self.ss, lambda e, p=ps, mc=mc:
                                       e.activation(
                                           out=self.sq_scr[:, 0:TC], in_=p[:],
                                           func=AF.Silu,
                                           bias=self.b_in_s[:, 16 + mc:17 + mc]))
                        self.pset(ps, "s", vev)
                        self.wait("v", self.ss, vev)
                        vv = self.inc("v", self.vs, lambda e, mc=mc, jp=jp:
                                      e.tensor_tensor(
                                          xi[:, (jp * 16 + mc) * TC:
                                             (jp * 16 + mc + 1) * TC],
                                          xi[:, (jp * 16 + mc) * TC:
                                             (jp * 16 + mc + 1) * TC],
                                          self.sq_scr[:, 0:TC], ALU.mult))
                        self.wait("s", self.vs, vv)
                used_in(par, self.ts.n)
                self.barrier(2, "D")

            self.barrier(1)
            vv_all = self.vs.n
            for mo in range(8):
                par, vld = ld_out(mo)
                self.waitv("t", vld)
                self.wait("t", self.vs, vv_all)
                for jp, j in enumerate((j0, j1)):
                    jg = j
                    op = (mo * 2 + jp) % 2
                    ps = (self.pA, self.pB)[op]
                    prs = [(self.stat_ap(self.wsl_c, par, 16 * 128, 128, kc, 0),
                            xi[:, (jp * 16 + kc) * TC:(jp * 16 + kc + 1) * TC])
                           for kc in range(16)]
                    lm = self.mm_group(ps, prs)
                    self.wait("s", self.ts, lm)
                    if pst_free[op]:
                        self.waitv("s", pst_free[op])
                    vev = self.inc("s", self.ss, lambda e, p=ps, op=op:
                                   e.activation(
                                       out=pstage[:, op * TC:(op + 1) * TC],
                                       in_=p[:], func=AF.Copy))
                    self.pset(ps, "s", vev)
                    self.wait("y", self.ss, vev)
                    vstore = self.inc("y", self.pst[op],
                                      lambda e, jg=jg, mo=mo, op=op:
                                      e.dma_start(
                                          out=self.pdram.ap()[:, (jg * 8 + mo) * TC:
                                                              (jg * 8 + mo + 1) * TC],
                                          in_=pstage[:, op * TC:(op + 1) * TC]))
                    pst_free[op] = (self.pst[op], vstore)
                used_out(par, self.ts.n)
                self.barrier(2, "E")
            self.lnbf_used[0] = self.ts.n
            self.lnbf_used[1] = self.ts.n
            if pair < 3:
                vst_pre = self.emit_stats_pair(j0 + 2, j1 + 2)

        p_stored_val = (self.pst[0].n, self.pst[1].n)
        self.barrier(1)
        self.emit_carry_final(sloc, apfx, scan_done)
        self.barrier(1)
        self.emit_compose_group(0, 2, sloc, apfx, scorr, pstage,
                                pst_free, p_stored_val)
        self.emit_compose_group(1, 3, sloc, apfx, scorr, pstage,
                                pst_free, p_stored_val)
        sc.close()

    def _stage_carry(self, src_ap, dram_in, dram_out, recv_dst, recv_hi):
        v1 = self.inc("g", self.ccb, lambda e, s=src_ap, d=dram_in:
                      e.dma_start(out=d.ap(), in_=s))
        self.wait("g", self.ccb, v1)
        vcc = self.inc("g", self.cc, lambda e, i=dram_in, o=dram_out:
                       e.collective_compute("AllGather", ALU.bypass,
                                            replica_groups=GROUPS,
                                            ins=[i.ap().opt()],
                                            outs=[o.ap().opt()]))
        self.wait("g", self.cc, vcc)
        sl = slice(DS, 2 * DS) if recv_hi else slice(0, DS)
        return self.inc("g", self.ccb, lambda e, o=dram_out, r=recv_dst, sl=sl:
                        e.dma_start(out=r[:], in_=o.ap()[sl, :]))

    def emit_carry_final(self, sloc, apfx, scan_done):
        self.wait("v", self.vs, scan_done[7])
        self.wait("v", self.cst, self.v_bias)
        vtf = self.inc("v", self.vs, lambda e, sloc=sloc, apfx=apfx:
                       e.scalar_tensor_tensor(
                           self.cvec[:, 1:2], apfx[:, 8 * TC - 1:8 * TC],
                           sloc[:, 4 * TC - 1:4 * TC],
                           sloc[:, 8 * TC - 1:8 * TC],
                           ALU.mult, ALU.add))
        self.wait("g", self.vs, vtf)
        vrx = self._stage_carry(self.cvec[:, 1:2], self.agx_in, self.agx_out,
                                self.recvX, False)
        self.wait("v", self.ccb, vrx)
        self.emit("v", lambda e: e.tensor_scalar(
            self.cvec[:, 2:3], self.recvX[:], self.nf_s[:], None, ALU.mult))
        self.inc("v", self.vs, lambda e, sloc=sloc, apfx=apfx:
                 e.scalar_tensor_tensor(
                     self.cvec[:, 3:4], apfx[:, 4 * TC - 1:4 * TC],
                     self.cvec[:, 2:3], sloc[:, 4 * TC - 1:4 * TC],
                     ALU.mult, ALU.add))

    def emit_compose_group(self, grp, ccol, sloc, apfx, scorr, pstage,
                           pst_free, p_stored_val):
        cv = self.vs.n
        for jj in range(4):
            j = grp * 4 + jj
            jg = j
            self.wait("v", self.vs, cv)
            vsc = self.inc("v", self.vs, lambda e, j=j, ccol=ccol, sloc=sloc,
                           apfx=apfx, scorr=scorr:
                           e.scalar_tensor_tensor(
                               scorr[:], apfx[:, j * TC:(j + 1) * TC],
                               self.cvec[:, ccol:ccol + 1],
                               sloc[:, j * TC:(j + 1) * TC],
                               ALU.mult, ALU.add))
            self.pwait(self.pROW)
            self.wait("t", self.vs, vsc)
            vmm = self.inc("t", self.ts, lambda e, scorr=scorr: e.matmul(
                self.pROW[64:65, :], self.ones64f[:], scorr[:],
                start=True, stop=True, skip_group_check=True))
            self.wait("s", self.ts, vmm)
            vrow = self.inc("s", self.ss, lambda e: e.activation(
                out=self.rowB[:], in_=self.pROW[64:65, :], func=AF.Copy))
            self.pset(self.pROW, "s", vrow)
            self.pwait(self.pBC3)
            self.wait("t", self.ss, vrow)
            vbc = self.inc("t", self.ts, lambda e: e.matmul(
                self.pBC3[:], self.onesbcf[:], self.rowB[:],
                start=True, stop=True, skip_group_check=True))
            self.wait("v", self.ts, vbc)
            vcm = 0
            for mc in range(8):
                op = mc % 2
                self.wait("y", self.pst[0], p_stored_val[0])
                self.wait("y", self.pst[1], p_stored_val[1])
                if pst_free[op]:
                    self.waitv("y", pst_free[op])
                vload = self.inc("y", self.pld[op], lambda e, jg=jg, mc=mc,
                                 op=op, pstage=pstage:
                                 e.dma_start(
                                     out=pstage[:, op * TC:(op + 1) * TC],
                                     in_=self.pdram.ap()[:, (jg * 8 + mc) * TC:
                                                         (jg * 8 + mc + 1) * TC]))
                self.wait("v", self.pld[op], vload)
                self.emit("v", lambda e, op=op, pstage=pstage: e.tensor_tensor(
                    self.cscr[:], pstage[:, op * TC:(op + 1) * TC],
                    self.pBC3[:], ALU.mult))
                vcm = self.inc("v", self.vs, lambda e, mc=mc, j=j:
                               e.scalar_tensor_tensor(
                                   self.st_ch(mc, j), self.cscr[:],
                                   self.b_out_s[:, mc:mc + 1],
                                   self.st_ch(mc, j), ALU.add, ALU.add))
                pst_free[op] = (self.vs, vcm)
            self.pset(self.pBC3, "v", vcm)

    def emit_ffn(self, final=False):
        di = self.di
        sc = ExitStack()
        hbuf = sc.enter_context(self.nc.sbuf_tensor(
            "hbuf_f" if final else "hbuf", [128, 2 * 24 * TC], BF16))
        if final:
            ostage = sc.enter_context(self.nc.sbuf_tensor(
                "ostage", [128, 2 * TC], F32))
        self.wsl_c = sc.enter_context(self.nc.sbuf_tensor(
            "wsl_cf%d" % (1 if final else 0), [128, 2 * 24 * 128], BF16))
        ld_g, used_g = self.slab_loader(self.wsl_a, 8 * 128, di["wg_t"], 8, 128, nbuf=4)
        ld_u, used_u = self.slab_loader(self.wsl_b, 8 * 128, di["wu_t"], 8, 128, nbuf=4)
        ld_d, used_d = self.slab_loader(self.wsl_c, 24 * 128, di["wd_t"], 24, 128)
        ost_free = [0, 0]
        vst_pre = None
        for pair in range(4):
            j0, j1 = 2 * pair, 2 * pair + 1
            vst = vst_pre if vst_pre is not None \
                else self.emit_stats_pair(j0, j1)
            vln = [self.emit_normalize_tile(j0, 0, self.ss, vst),
                   self.emit_normalize_tile(j1, 1, self.ss, vst)]
            for mh in range(24):
                parg, vldg = ld_g(mh)
                paru, vldu = ld_u(mh)
                self.waitv("t", vldg); self.waitv("t", vldu)
                for jp in (0, 1):
                    self.wait("t", self.vs, vln[jp])
                    prs = [(self.stat_ap(self.wsl_a, parg, 8 * 128, 128, kc, 0),
                            self.ln_bf[:, (jp * 8 + kc) * TC:
                                       (jp * 8 + kc + 1) * TC])
                           for kc in range(8)]
                    lg = self.mm_group(self.pA, prs)
                    prs = [(self.stat_ap(self.wsl_b, paru, 8 * 128, 128, kc, 0),
                            self.ln_bf[:, (jp * 8 + kc) * TC:
                                       (jp * 8 + kc + 1) * TC])
                           for kc in range(8)]
                    lu = self.mm_group(self.pB, prs)
                    self.wait("s", self.ts, lg)
                    self.wait("s", self.vs, self.vs.n)
                    vsg = self.inc("s", self.ss, lambda e: e.activation(
                        out=self.sq_scr[:, 0:TC], in_=self.pA[:],
                        func=AF.Silu, scale=QS))
                    self.pset(self.pA, "s", vsg)
                    self.wait("v", self.ss, vsg)
                    self.wait("v", self.ts, lu)
                    vh = self.inc("v", self.vs, lambda e, mh=mh, jp=jp,
                                  hbuf=hbuf: e.tensor_tensor(
                                      hbuf[:, (jp * 24 + mh) * TC:
                                           (jp * 24 + mh + 1) * TC],
                                      self.sq_scr[:, 0:TC], self.pB[:],
                                      ALU.mult))
                    self.pset(self.pB, "v", vh)
                    self.wait("s", self.vs, vh)
                used_g(parg, self.ts.n)
                used_u(paru, self.ts.n)
            vh_all = self.vs.n
            for md in range(8):
                par, vld = ld_d(md)
                self.waitv("t", vld)
                self.wait("t", self.vs, vh_all)
                for jp, j in enumerate((j0, j1)):
                    ps = (self.pA, self.pB)[jp]
                    prs = [(self.stat_ap(self.wsl_c, par, 24 * 128, 128, kc, 0),
                            hbuf[:, (jp * 24 + kc) * TC:(jp * 24 + kc + 1) * TC])
                           for kc in range(24)]
                    lm = self.mm_group(ps, prs)
                    self.wait("v", self.ts, lm)
                    if not final:
                        vres = self.inc("v", self.vs,
                                        lambda e, p=ps, md=md, j=j:
                                        e.scalar_tensor_tensor(
                                            self.st_ch(md, j), p[:], QS * QS,
                                            self.st_ch(md, j), ALU.mult,
                                            ALU.add))
                    else:
                        op = (md * 2 + jp) % 2
                        if ost_free[op]:
                            self.wait("v", self.ost[op], ost_free[op])
                        vres = self.inc("v", self.vs,
                                        lambda e, p=ps, md=md, j=j, op=op,
                                        ostage=ostage:
                                        e.scalar_tensor_tensor(
                                            ostage[:, op * TC:(op + 1) * TC],
                                            p[:], QS * QS, self.st_ch(md, j),
                                            ALU.mult, ALU.add))
                        self.wait("y", self.vs, vres)
                        ost_free[op] = self.inc(
                            "y", self.ost[op], lambda e, md=md, j=j, op=op,
                            ostage=ostage: e.dma_start(
                                out=self.yt[md * 128:(md + 1) * 128,
                                            j * TC:(j + 1) * TC],
                                in_=ostage[:, op * TC:(op + 1) * TC]))
                    self.pset(ps, "v", vres)
                used_d(par, self.ts.n)
            self.lnbf_used[0] = self.ts.n
            self.lnbf_used[1] = self.ts.n
            if pair < 3:
                vst_pre = self.emit_stats_pair(j0 + 2, j1 + 2)
        sc.close()

    def emit_attn(self):
        di = self.di
        self.sc_attn = ExitStack()
        NPAR = 3
        self.wcos = [self.sc_attn.enter_context(
            self.nc.sbuf_tensor("wcos%d" % i, [128, 2 * TC], F32))
            for i in range(NPAR)]
        self.wsin = [self.sc_attn.enter_context(
            self.nc.sbuf_tensor("wsin%d" % i, [128, 2 * TC], F32))
            for i in range(NPAR)]
        self.ibuf = self.sc_attn.enter_context(
            self.nc.sbuf_tensor("ibuf", [128, 2 * TC], I32))
        self.fs_vdone = [0] * NPAR
        self.fs_v2 = [0] * NPAR
        self.fs_npar = NPAR
        self.fs_par = 0
        kvS = self.sc_attn.enter_context(
            self.nc.sbuf_tensor("kvS", [128, NH * 2 * DH], BF16))
        sc1 = ExitStack()
        kfm = sc1.enter_context(self.nc.sbuf_tensor("kfm", [128, 2 * 8 * TC], BF16))
        vtm = sc1.enter_context(self.nc.sbuf_tensor("vtm", [128, 2 * 4 * D], BF16))
        kp = sc1.enter_context(self.nc.sbuf_tensor("kp", [128, NH * 256], BF16))
        wvsl = sc1.enter_context(self.nc.sbuf_tensor("wvsl", [128, 2 * 8 * TC], BF16))

        ld_k, used_k = self.slab_loader(self.wsl_b, 8 * 128, di["wk_t"], 8, 128, nbuf=4)
        ld_v, used_v = self.slab_loader(wvsl, 8 * TC, di["wv_t"], 8, TC)
        kv_armed = set()
        kp_used = {0: 0, 1: 0}
        vkv = 0

        def drain_kv(pd):
            nonlocal vkv
            ppair, pjp, ptc, pu, ppar = pd
            base = pu * 8 * 256
            cos_out = kp[:, base:base + 8 * 256].rearrange(
                "p (h m) -> p h m", h=8)[:, :, 0:128]
            sin_out = kp[:, base:base + 8 * 256].rearrange(
                "p (h m) -> p h m", h=8)[:, :, 128:256]
            vb = self.foursin_b(
                cos_out, sin_out, 2 * TC, ppar,
                out_guard=(self.ts, kp_used[pu]))
            if self.attn_stop < 14:
                return
            self.wait("t", self.vs, vb)
            last_tc = (ppair == 3 and pjp == 1 and ptc == 3)
            for hh in range(8):
                h = pu * 8 + hh
                bank = self.kv_banks[h // 4]
                pb = (h % 2) * 64
                cb = ((h % 4) // 2) * 256
                key = (h // 4, pb)
                st_f = key not in kv_armed
                kv_armed.add(key)
                statp = vtm[:, (pjp * 4 + ptc) * D + h * DH:
                            (pjp * 4 + ptc) * D + (h + 1) * DH]
                mov = kp[:, h * 256:(h + 1) * 256]
                if hh < 7:
                    self.emit("t", lambda e, s=statp, m=mov, bank=bank,
                              pb=pb, cb=cb, a=st_f, b=last_tc:
                              e.matmul(bank[pb:pb + DS, cb:cb + 256],
                                       s, m, start=a, stop=b,
                                       skip_group_check=True))
                else:
                    vkv = self.inc("t", self.ts,
                                   lambda e, s=statp, m=mov, bank=bank,
                                   pb=pb, cb=cb, a=st_f, b=last_tc:
                                   e.matmul(bank[pb:pb + DS,
                                                 cb:cb + 256],
                                            s, m, start=a, stop=b,
                                            skip_group_check=True))
            kp_used[pu] = vkv

        pq1 = []
        for pair in range(4):
            j0, j1 = 2 * pair, 2 * pair + 1
            vst = self.emit_stats_pair(j0, j1, rowbank=self.pBC1)
            vln = [self.emit_normalize_tile(j0, 0, self.ss, vst),
                   self.emit_normalize_tile(j1, 1, self.ss, vst)]
            kfm_guard = self.ts.n if pair > 0 else 0
            for mk in range(8):
                par, vld = ld_k(mk)
                self.waitv("t", vld)
                for jp in (0, 1):
                    self.wait("t", self.vs, vln[jp])
                    ps = (self.pA, self.pB)[(mk * 2 + jp) % 2]
                    prs = [(self.stat_ap(self.wsl_b, par, 8 * 128, 128, kc, 0),
                            self.ln_bf[:, (jp * 8 + kc) * TC:
                                       (jp * 8 + kc + 1) * TC])
                           for kc in range(8)]
                    lm = self.mm_group(ps, prs)
                    self.wait("s", self.ts, lm)
                    self.wait("s", self.ts, kfm_guard)
                    vev = self.inc("s", self.ss, lambda e, p=ps, mk=mk, jp=jp,
                                   kfm=kfm: e.activation(
                                       out=kfm[:, (jp * 8 + mk) * TC:
                                               (jp * 8 + mk + 1) * TC],
                                       in_=p[:], func=AF.Identity,
                                       bias=self.bk_s[:, mk:mk + 1]))
                    self.pset(ps, "s", vev)
                used_k(par, self.ts.n)
            for pd in pq1:
                drain_kv(pd)
            pq1 = []
            vtm_guard = self.ts.n if pair > 0 else 0
            if self.attn_stop < 12:
                continue
            self.wait("s", self.gw, self.v_setup_gw)
            for half in range(2):
                parv, vldv = ld_v(half)
                self.waitv("t", vldv)
                for jp in (0, 1):
                    for tc in range(4):
                        ps = (self.pA, self.pB)[(tc + jp) % 2]
                        self.pwait(ps)
                        self.emit("t", lambda e, p=ps, half=half: e.matmul(
                            p[:], self.ones1_bf[:],
                            self.bv_bf[:, half * TC:(half + 1) * TC],
                            start=True, stop=False, skip_group_check=True))
                        lm = 0
                        for kc in range(8):
                            statp = self.ln_bf[:, (jp * 8 + kc) * TC + tc * 128:
                                               (jp * 8 + kc) * TC + (tc + 1) * 128]
                            mov = wvsl[:, parv * (8 * TC) + kc * TC:
                                       parv * (8 * TC) + (kc + 1) * TC]
                            if kc < 7:
                                self.emit("t", lambda e, p=ps, s=statp, m=mov:
                                          e.matmul(p[:], s, m, start=False,
                                                   stop=False,
                                                   skip_group_check=True))
                            else:
                                lm = self.inc("t", self.ts,
                                              lambda e, p=ps, s=statp, m=mov:
                                              e.matmul(p[:], s, m, start=False,
                                                       stop=True,
                                                       skip_group_check=True))
                        self.wait("s", self.ts, lm)
                        self.wait("s", self.ts, vtm_guard)
                        vev = self.inc("s", self.ss,
                                       lambda e, p=ps, jp=jp, tc=tc, half=half,
                                       vtm=vtm: e.activation(
                                           out=vtm[:, (jp * 4 + tc) * D +
                                                   half * TC:
                                                   (jp * 4 + tc) * D +
                                                   (half + 1) * TC],
                                           in_=p[:], func=AF.Copy))
                        self.pset(ps, "s", vev)
                used_v(parv, self.ts.n)
            if self.attn_stop < 13:
                continue
            for jp, j in enumerate((j0, j1)):
                for tc in range(4):
                    for u in range(2):
                        par = self.fs_par
                        self.fs_par = (self.fs_par + 1) % self.fs_npar
                        lm = 0
                        for uu in range(2):
                            hg = 2 * u + uu
                            ps = (self.pA, self.pB)[uu]
                            self.pwait(ps)
                            for hh in range(4):
                                h = hg * 4 + hh
                                g = h // 2
                                statp = kfm[:, (jp * 8 + g) * TC + tc * 128:
                                            (jp * 8 + g) * TC +
                                            (tc + 1) * 128]
                                mov = self.projz[:, (h % 2) * KM:
                                                 (h % 2 + 1) * KM]
                                if uu == 1 and hh == 3:
                                    lm = self.inc(
                                        "t", self.ts,
                                        lambda e, p=ps, s=statp, m=mov, hh=hh:
                                        e.matmul(
                                            p[:, hh * 128:(hh + 1) * 128],
                                            s, m, start=False, stop=True,
                                            skip_group_check=True))
                                else:
                                    self.emit(
                                        "t",
                                        lambda e, p=ps, s=statp, m=mov, hh=hh,
                                        a=(hh == 0), b=(hh == 3):
                                        e.matmul(
                                            p[:, hh * 128:(hh + 1) * 128],
                                            s, m, start=a, stop=b,
                                            skip_group_check=True))
                        if len(pq1) == 2:
                            drain_kv(pq1.pop(0))
                        vu = self.foursin_a(
                            [(self.pA[:], 0), (self.pB[:], TC)],
                            2 * TC, par, psum_wait=lm)
                        self.pset(self.pA, "s", vu)
                        self.pset(self.pB, "s", vu)
                        pq1.append((pair, jp, tc, u, par))
            self.lnbf_used[0] = self.ts.n
            self.lnbf_used[1] = self.ts.n

        if self.attn_stop < 14:
            sc1.close()
            self.sc_attn.close()
            return
        if self.attn_stop < 2:
            sc1.close()
            self.sc_attn.close()
            return
        for pd in pq1:
            drain_kv(pd)
        pq1 = []
        sc1.close()
        sc2 = ExitStack()
        kvp = sc2.enter_context(self.nc.sbuf_tensor("kvp", [128, 2048], F32))
        self.wait("s", self.ts, vkv)
        vev = 0
        for q in range(4):
            vev = self.inc("s", self.ss, lambda e, q=q, kvp=kvp: e.activation(
                out=kvp[:, q * TC:(q + 1) * TC],
                in_=self.kv_banks[q][:], func=AF.Copy))
            self.pset(self.kv_banks[q], "s", vev)
        self.wait("g", self.ss, vev)
        v1 = self.inc("g", self.ccb, lambda e, kvp=kvp: e.dma_start(
            out=self.kv_in.ap(), in_=kvp[:]))
        self.wait("g", self.ccb, v1)
        vcc = self.inc("g", self.cc, lambda e: e.collective_compute(
            "AllReduce", ALU.add, replica_groups=GROUPS,
            ins=[self.kv_in.ap().opt()], outs=[self.kv_out.ap().opt()]))
        self.wait("g", self.cc, vcc)
        v2 = self.inc("g", self.ccb, lambda e, kvp=kvp: e.dma_start(
            out=kvp[:], in_=self.kv_out.ap()))
        if self.dbg:
            self.wait("g", self.ccb, v2)
            vdb = self.inc("g", self.gw, lambda e, kvp=kvp: e.dma_start(
                out=self.dbg_kv, in_=kvp[:]))
            vdb = self.inc("g", self.gw, lambda e, kp=kp: e.dma_start(
                out=self.dbg_kp, in_=kp[:, NH * 256:2 * NH * 256]))
        self.wait("t", self.ccb, v2)
        self.wait("t", self.cst, self.v_bias)
        for h in range(NH):
            q = h // 4
            pb = (h % 2) * 64
            cb = ((h % 4) // 2) * 256
            for c in range(2):
                self.pwait(self.pA)
                vt = self.inc("t", self.ts, lambda e, q=q, pb=pb, cb=cb, c=c,
                              kvp=kvp: e.transpose(
                                  self.pA[:, 0:DH],
                                  kvp[pb:pb + DS, q * TC + cb + c * 128:
                                      q * TC + cb + (c + 1) * 128],
                                  self.eye_f[pb:pb + DS, :]))
                self.wait("s", self.ts, vt)
                vev = self.inc("s", self.ss, lambda e, h=h, c=c, kvS=kvS:
                               e.activation(
                                   out=kvS[:, (h * 2 + c) * DH:
                                           (h * 2 + c + 1) * DH],
                                   in_=self.pA[:, 0:DH], func=AF.Copy))
                self.pset(self.pA, "s", vev)
        self.v_kvS = self.ss.n
        sc2.close()
        if self.attn_stop < 3:
            self.sc_attn.close()
            return

        sc3 = ExitStack()
        qfm = sc3.enter_context(self.nc.sbuf_tensor("qfm", [128, 2 * 8 * TC], BF16))
        qph = sc3.enter_context(self.nc.sbuf_tensor("qph", [128, 2 * 4 * TC], BF16))
        afm = sc3.enter_context(self.nc.sbuf_tensor("afm", [128, 3 * 8 * TC], BF16))
        ld_q, used_q = self.slab_loader(self.wsl_b, 8 * 128, di["wq_t"], 8, 128, nbuf=4)
        ld_o, used_o = self.slab_loader(self.wsl_a, 8 * 128, di["wo_t"], 8, 128, nbuf=4)
        qfm_used = [0, 0]
        qph_used = [0, 0]
        self.wait("t", self.ss, self.v_kvS)

        vaf = 0
        pq = []
        afm_pend = []

        def drain_afm():
            nonlocal vaf
            php, eb, lm, tp = afm_pend.pop(0)
            self.wait("s", self.ts, lm)
            vaf = self.inc("s", self.ss, lambda e, hp=php, afm=afm, tp=tp,
                           eb=eb: e.activation(
                               out=afm[:, (tp * 8 + hp) * TC:
                                       (tp * 8 + hp + 1) * TC],
                               in_=eb[:], func=AF.Copy,
                               scale=1.0 / KM))
            self.pset(eb, "s", vaf)

        def drain_q(pd):
            ptp, php, ppar = pd
            qp = php % 2
            eb = (self.pBC3, self.pROW)[qp]
            q0 = qp * 4 * TC
            if len(afm_pend) == 2:
                drain_afm()
            vb = self.foursin_b(
                qph[:, q0:q0 + 2 * TC],
                qph[:, q0 + 2 * TC:q0 + 4 * TC],
                2 * TC, ppar, out_guard=(self.ts, qph_used[qp]))
            self.wait("t", self.vs, vb)
            self.pwait(eb)
            lm = 0
            for hh in range(2):
                h = php * 2 + hh
                r0 = hh * 64
                for c in range(2):
                    statp = kvS[:, (h * 2 + c) * DH:(h * 2 + c + 1) * DH]
                    mov = qph[:, q0 + c * 2 * TC + hh * TC:
                              q0 + c * 2 * TC + (hh + 1) * TC]
                    if hh == 1 and c == 1:
                        lm = self.inc("t", self.ts,
                                      lambda e, s=statp, m=mov, r0=r0,
                                      eb=eb:
                                      e.matmul(eb[r0:r0 + DS, :], s, m,
                                               start=False, stop=True,
                                               skip_group_check=True))
                    else:
                        self.emit("t", lambda e, s=statp, m=mov, r0=r0,
                                  eb=eb, a=(c == 0), b=(c == 1):
                                  e.matmul(eb[r0:r0 + DS, :], s, m,
                                           start=a, stop=b,
                                           skip_group_check=True))
            qph_used[qp] = lm
            afm_pend.append((php, eb, lm, ptp))

        def emit_wo_group(jj, mo):
            par, vld = ld_o(mo)
            self.waitv("t", vld)
            self.wait("t", self.ss, vaf)
            ps = (self.pA, self.pB)[mo % 2]
            prs = [(self.stat_ap(self.wsl_a, par, 8 * 128, 128, kc, 0),
                    afm[:, ((jj % 3) * 8 + kc) * TC:
                        ((jj % 3) * 8 + kc + 1) * TC])
                   for kc in range(8)]
            lm = self.mm_group(ps, prs)
            self.wait("v", self.ts, lm)
            vres = self.inc("v", self.vs, lambda e, p=ps, mo=mo, j=jj:
                            e.scalar_tensor_tensor(
                                self.st_ch(mo, j), p[:],
                                self.bo_s[:, mo:mo + 1],
                                self.st_ch(mo, j), ALU.add, ALU.add))
            self.pset(ps, "v", vres)
            used_o(par, self.ts.n)

        def emit_wo(jj):
            for mo in range(8):
                emit_wo_group(jj, mo)

        for j in range(NT):
            jp = j % 2
            vst = self._stats_one_tile(
                j, self.mean_pr[:, jp * TC:(jp + 1) * TC],
                self.rstd_pr[:, jp * TC:(jp + 1) * TC], rowbank=self.pBC1)
            vln = self.emit_normalize_tile(j, jp, self.ss, vst)
            vev_q = [0] * 8
            for mq in range(8):
                par, vld = ld_q(mq)
                self.waitv("t", vld)
                self.wait("t", self.vs, vln)
                ps = (self.pA, self.pB)[mq % 2]
                prs = [(self.stat_ap(self.wsl_b, par, 8 * 128, 128, kc, 0),
                        self.ln_bf[:, (jp * 8 + kc) * TC:
                                   (jp * 8 + kc + 1) * TC])
                       for kc in range(8)]
                lm = self.mm_group(ps, prs)
                self.wait("s", self.ts, lm)
                self.wait("s", self.ts, qfm_used[jp])
                vev = self.inc("s", self.ss, lambda e, p=ps, mq=mq, jp=jp,
                               qfm=qfm: e.activation(
                                   out=qfm[:, (jp * 8 + mq) * TC:
                                           (jp * 8 + mq + 1) * TC],
                                   in_=p[:], func=AF.Identity,
                                   bias=self.bq_s[:, mq:mq + 1]))
                self.pset(ps, "s", vev)
                vev_q[mq] = vev
                used_q(par, self.ts.n)
            self.lnbf_used[jp] = self.ts.n
            wo_q = [(j - 2, mo) for mo in range(8)] if j > 1 else []
            for hp in range(8):
                par = self.fs_par
                self.fs_par = (self.fs_par + 1) % self.fs_npar
                vb0, vb1 = ((self.pA, self.pB), (self.pC, self.pD))[hp % 2]
                lm = 0
                for hh in range(2):
                    h = hp * 2 + hh
                    ps = (vb0, vb1)[hh]
                    self.pwait(ps)
                    self.wait("t", self.ss, vev_q[hp])
                    fn = lambda e, p=ps, h=h, jp=jp, g=hp, qfm=qfm: e.matmul(
                        p[:], self.projz[:, (h % 2) * KM:(h % 2 + 1) * KM],
                        qfm[:, (jp * 8 + g) * TC:(jp * 8 + g + 1) * TC],
                        start=True, stop=True, skip_group_check=True)
                    if hh == 1:
                        lm = self.inc("t", self.ts, fn)
                    else:
                        self.emit("t", fn)
                if len(pq) == 2:
                    drain_q(pq.pop(0))
                vu = self.foursin_a([(vb0[:], 0), (vb1[:], TC)],
                                    2 * TC, par, psum_wait=lm)
                self.pset(vb0, "s", vu)
                self.pset(vb1, "s", vu)
                if wo_q:
                    emit_wo_group(*wo_q.pop(0))
                pq.append((j % 3, hp, par))
            qfm_used[jp] = self.ts.n
            if self.dbg and j == NT - 1:
                self.wait("g", self.ss, self.ss.n)
                self.inc("g", self.gw, lambda e, qph=qph: e.dma_start(
                    out=self.dbg_qph, in_=qph[:, TC:2 * TC]) if False else
                    e.dma_start(out=self.dbg_qph, in_=qph[:, 0:2 * TC]))
                self.inc("g", self.gw, lambda e, afm=afm: e.dma_start(
                    out=self.dbg_afm, in_=afm[:]))
        for pd in pq:
            drain_q(pd)
        while afm_pend:
            drain_afm()
        emit_wo(NT - 2)
        emit_wo(NT - 1)
        sc3.close()
        self.sc_attn.close()

    def foursin_a(self, srcs, ncol, par, psum_wait):
        wc, ws, ib = self.wcos[par], self.wsin[par], self.ibuf
        self.wait("s", self.vs, self.fs_vdone[par])
        self.wait("s", self.ts, psum_wait)
        vu = 0
        for p, off in srcs:
            vu = self.inc("s", self.ss, lambda e, p=p, off=off, wc=wc:
                          e.activation(out=wc[:, off:off + TC], in_=p,
                                       func=AF.Copy, scale=INV16PI))
        self.wait("v", self.ss, vu)
        self.emit("v", lambda e, wc=wc, ib=ib, n=ncol:
                  e.tensor_copy(ib[:, 0:n], wc[:, 0:n]))
        self.fs_v2[par] = self.inc(
            "v", self.vs, lambda e, wc=wc, ws=ws, ib=ib, n=ncol:
            e.tensor_tensor(ws[:, 0:n], wc[:, 0:n], ib[:, 0:n], ALU.subtract))
        return vu

    def foursin_b(self, cos_out, sin_out, ncol, par, out_guard=None):
        wc, ws = self.wcos[par], self.wsin[par]
        self.wait("s", self.vs, self.fs_v2[par])
        self.emit("s", lambda e, wc=wc, ws=ws, n=ncol: e.activation(
            out=wc[:, 0:n], in_=ws[:, 0:n], func=AF.Abs))
        self.emit("s", lambda e, ws=ws, n=ncol: e.activation(
            out=ws[:, 0:n], in_=ws[:, 0:n], func=AF.Sin, scale=math.pi))
        vs1 = self.inc("s", self.ss, lambda e, wc=wc, n=ncol: e.activation(
            out=wc[:, 0:n], in_=wc[:, 0:n], func=AF.Sin, scale=-math.pi,
            bias=self.c_hpi[:]))
        self.wait("v", self.ss, vs1)
        if out_guard is not None:
            self.waitv("v", out_guard)
        self.emit("v", lambda e, o=sin_out, wc=wc, ws=ws, n=ncol:
                  e.scalar_tensor_tensor(o, ws[:, 0:n], 2.0, wc[:, 0:n],
                                         ALU.mult, ALU.mult))
        self.emit("v", lambda e, wc=wc, ws=ws, n=ncol:
                  e.scalar_tensor_tensor(wc[:, 0:n], ws[:, 0:n], -2.0,
                                         ws[:, 0:n], ALU.mult, ALU.mult))
        v = self.inc("v", self.vs, lambda e, o=cos_out, wc=wc, n=ncol:
                     e.tensor_scalar(o, wc[:, 0:n], 1.0, None, ALU.add))
        self.fs_vdone[par] = v
        return v

    def emit_dump_stream(self):
        sc = ExitStack()
        dump = sc.enter_context(self.nc.sbuf_tensor("dump_f32", [128, T], F32))
        self.wait("s", self.vs, self.vs.n)
        self.wait("s", self.ts, self.ts.n)
        for mc in range(8):
            vc = self.inc("s", self.ss, lambda e, mc=mc, dump=dump:
                          e.activation(out=dump[:], in_=self.st_ch(mc),
                                       func=AF.Copy))
            self.wait("y", self.ss, vc)
            vst = self.inc("y", self.sy, lambda e, mc=mc, dump=dump:
                           e.dma_start(out=self.yt[mc * 128:(mc + 1) * 128, :],
                                       in_=dump[:]))
            self.wait("s", self.sy, vst)
        sc.close()

    def build(self):
        self.emit_setup()
        if self.upto >= 1:
            self.emit_mamba()
        if self.upto >= 2:
            self.emit_ffn(final=False)
        if self.upto >= 3:
            self.emit_attn()
        if self.upto >= 4:
            self.emit_ffn(final=True)
        else:
            self.emit_dump_stream()
        self.wait("g", self.sy, self.sy.n)
        self.wait("g", self.ost[0], self.ost[0].n)
        self.wait("g", self.ost[1], self.ost[1].n)

        block = self.es.enter_context(self.nc.Block())

        @block.sync
        def _(e):
            for fn in self.prog["y"]:
                fn(e)

        @block.gpsimd
        def _(e):
            for fn in self.prog["g"]:
                fn(e)

        @block.tensor
        def _(e):
            for fn in self.prog["t"]:
                fn(e)

        @block.scalar
        def _(e):
            for fn in self.prog["s"]:
                fn(e)

        @block.vector
        def _(e):
            for fn in self.prog["v"]:
                fn(e)

        self.es.close()
        return self.nc


def build_nc(upto=4, dbg=False, paranoid=0, rdbg=False, attn_stop=99):
    return Builder(upto=upto, dbg=dbg, paranoid=paranoid, rdbg=rdbg,
                   attn_stop=attn_stop).build()


def slab_major(W, kc, cols):
    K, M = W.shape
    assert K == kc * 128 and M % cols == 0
    t = W.reshape(kc, 128, M // cols, cols).transpose(1, 2, 0, 3)
    return np.ascontiguousarray(t.reshape(128, -1))


def make_in_maps(inputs):
    x = inputs["L0_attn_a_x"]
    C = np.ascontiguousarray

    def f32(a):
        return np.asarray(a, dtype=np.float32)
    base = {
        "w_in": slab_major(f32(inputs["L0_attn_a_W_in"]), 8, 256),
        "b_in_r": C(f32(inputs["L0_attn_a_b_in"]).reshape(32, 128).T),
        "w_dt": slab_major(f32(inputs["L0_attn_a_W_dt"]), 16, 128),
        "b_dt_r": C(f32(inputs["L0_attn_a_b_dt"]).reshape(8, 128).T),
        "w_xp": slab_major(f32(inputs["L0_attn_a_W_xp"]), 16, 64),
        "b_xp_r": C(f32(inputs["L0_attn_a_b_xp"]).reshape(DS, 1)),
        "w_out": slab_major(f32(inputs["L0_attn_a_W_out"]), 16, 128),
        "b_out_r": C(f32(inputs["L0_attn_a_b_out"]).reshape(8, 128).T),
        "a_log_r": C(f32(inputs["L0_attn_a_A_log"]).reshape(DS, 1)),
        "d_r": C(f32(inputs["L0_attn_a_D"]).reshape(8, 128).T),
        "proj": C(f32(inputs["L0_attn_b_proj_mat"])),
        "wq_t": slab_major(f32(inputs["L0_attn_b_Wq"]).T, 8, 128),
        "wk_t": slab_major(f32(inputs["L0_attn_b_Wk"]).T, 8, 128),
        "wv_t": slab_major(f32(inputs["L0_attn_b_Wv"]).T, 8, 512),
        "wo_t": slab_major(f32(inputs["L0_attn_b_Wo"]).T, 8, 128),
        "bq_r": C(f32(inputs["L0_attn_b_bq"]).reshape(8, 128).T),
        "bk_r": C(f32(inputs["L0_attn_b_bk"]).reshape(8, 128).T),
        "bo_r": C(f32(inputs["L0_attn_b_bo"]).reshape(8, 128).T),
        "bv_row": C(f32(inputs["L0_attn_b_bv"]).reshape(1, D)),
        "wg_t": slab_major(np.asarray(inputs["L0_ffn_w_gate"], np.int32).T, 8, 128),
        "wu_t": slab_major(np.asarray(inputs["L0_ffn_w_up"], np.int32).T, 8, 128),
        "wd_t": slab_major(np.asarray(inputs["L0_ffn_w_down"], np.int32).T, 24, 128),
        "eye": np.concatenate([np.eye(DS, dtype=np.float32),
                               np.eye(DS, dtype=np.float32)], axis=0),
    }
    in_maps = []
    for c in range(8):
        b, h = c // 2, c % 2
        m = dict(base)
        m["xt"] = C(f32(x[b, h * T:(h + 1) * T, :]).T)
        m["isfirst"] = np.full((DS, 1), 1.0 if h == 0 else 0.0, np.float32)
        m["notfirst"] = np.full((DS, 1), 0.0 if h == 0 else 1.0, np.float32)
        in_maps.append(m)
    return in_maps


def kernel(**inputs):
    nc = build_nc(upto=4)
    in_maps = make_in_maps(inputs)
    res = run_bass_kernel_spmd(nc, in_maps, core_ids=list(range(8)))
    out = np.empty((B, S, D), dtype=np.float32)
    for c in range(8):
        b, h = c // 2, c % 2
        out[b, h * T:(h + 1) * T, :] = np.asarray(res.results[c]["yt"]).T
    return out
